# version 26
# baseline (speedup 1.0000x reference)
import numpy as np
import contextlib
import concourse.bass as bass
import concourse.mybir as mybir
from concourse.alu_op_type import AluOpType as OP
from concourse.bass_utils import run_bass_kernel_spmd

F32 = mybir.dt.float32
AF = mybir.ActivationFunctionType
AX = mybir.AxisListType.X

B, L, D = 4, 4096, 1024
NCORES = 8
LH = L // 2
NPAIR = 8
LT, NLT = 512, 4
NCHT = 4
NTB = 16
EPS = 1e-6
NEXP = 10
IEXP = 512
BLOB = 65536 + 4 * 1024 + 16

ENGS = ("pe", "act", "dve", "pool", "sync")
STOP = 0
SKIP = set()


class Rec:
    def __init__(self):
        self.ops = {e: [] for e in ENGS}
        self.cnt = {e: 0 for e in ENGS}
        self.dma = {e: 0 for e in ENGS}
        self.cc = 0

    def op(self, eng, fn):
        self.ops[eng].append(("op", fn, None, None))

    def tick(self, eng, fn):
        self.cnt[eng] += 1
        self.ops[eng].append(("inc", fn, eng, self.cnt[eng]))
        return (eng, self.cnt[eng])

    def tick_last(self, eng):
        kind, fn, semn, val = self.ops[eng][-1]
        if kind == "inc":
            return (semn, val)
        assert kind == "op", kind
        self.cnt[eng] += 1
        self.ops[eng][-1] = ("inc", fn, eng, self.cnt[eng])
        return (eng, self.cnt[eng])

    def dma_op(self, eng, fn):
        self.dma[eng] += 16
        self.ops[eng].append(("dma", fn, eng + "_dma", self.dma[eng]))
        return (eng + "_dma", self.dma[eng])

    def cc_op(self, fn):
        self.cc += 1
        self.ops["pool"].append(("inc", fn, "cc", self.cc))
        return ("cc", self.cc)

    def bump_after_dma(self, eng):
        self.ops[eng].append(("wait", None, eng + "_dma", self.dma[eng]))
        self.cnt[eng] += 1
        self.ops[eng].append(("bump", None, eng, self.cnt[eng]))
        return (eng, self.cnt[eng])

    def await_dma(self, consumer, eng):
        tok = self.bump_after_dma(eng)
        self.wait(consumer, tok)
        return tok

    def fence_dma_at(self, eng, dtok):
        self.ops[eng].append(("wait", None, dtok[0], dtok[1]))
        self.cnt[eng] += 1
        self.ops[eng].append(("bump", None, eng, self.cnt[eng]))
        return (eng, self.cnt[eng])

    def wait(self, eng, token):
        if token is not None:
            self.ops[eng].append(("wait", None, token[0], token[1]))

    def drain(self, eng):
        self.ops[eng].append(("drain", None, None, None))

    def pe_tok(self):
        return ("pe", self.cnt["pe"])

    def dve_tok(self):
        return ("dve", self.cnt["dve"])

    def barrier(self):
        toks = []
        for e in ENGS:
            if self.dma[e]:
                self.wait(e, (e + "_dma", self.dma[e]))
            if e in ("pe", "act", "dve", "pool"):
                self.ops[e].append(("drain", None, None, None))
            self.cnt[e] += 1
            self.ops[e].append(("bump", None, e, self.cnt[e]))
            toks.append((e, self.cnt[e]))
        for e in ENGS:
            for t in toks:
                if t[0] != e:
                    self.wait(e, t)


class View:
    def __init__(self, t, off, cols, rows=128, row0=0):
        self.t, self.off, self.cols, self.rows, self.row0 = t, off, cols, rows, row0

    def __getitem__(self, idx):
        r, c = idx
        if isinstance(c, slice):
            c0 = 0 if c.start is None else c.start
            c1 = self.cols if c.stop is None else c.stop
        else:
            c0, c1 = c, c + 1
        if isinstance(r, slice):
            r = slice((r.start or 0) + self.row0,
                      (self.rows if r.stop is None else r.stop) + self.row0)
        else:
            r = slice(r + self.row0, r + self.row0 + 1)
        return self.t[r, self.off + c0:self.off + c1]

    def all(self):
        return self.t[self.row0:self.row0 + self.rows, self.off:self.off + self.cols]


def build_nc(n_cc=NCORES):
    nc = bass.Bass("TRN2", target_bir_lowering=False)
    dt = {}

    def din(name, shape):
        dt[name] = nc.dram_tensor(name, shape, F32, kind="ExternalInput")

    def dout(name, shape):
        dt[name] = nc.dram_tensor(name, shape, F32, kind="ExternalOutput")

    din("x_fm", [D, LH]); din("x_tm", [LH, D])
    din("w_q", [D, D]); din("w_k", [D, D]); din("w_v", [D, D]); din("w_o", [D, D])
    din("b_q", [128, 8]); din("b_k", [128, 8])
    din("b_v", [1, D]); din("b_o", [1, D])
    din("gw_t", [D, 8]); din("lb", [1, 8])
    din("ffn_gw", [NEXP, D, IEXP]); din("ffn_uw", [NEXP, D, IEXP]); din("ffn_dw", [NEXP, IEXP, D])
    din("c_id128", [128, 128])
    din("c_mask", [128, 128])
    din("c_obd16", [128, 128])
    din("c_selp16", [16, 1024])
    din("c_ones1", [1, 128]); din("c_onescol", [128, 1])
    din("c_norm", [48, LH])
    din("c_selcar", [128, n_cc]); din("c_one", [128, 1])
    dout("out", [LH, D])

    BF16 = mybir.dt.bfloat16
    scr_tm = nc.dram_tensor("scr_tm", [LH, D], F32)
    scr16_hi = nc.dram_tensor("scr16_hi", [D, LH], BF16)
    scr16_lo = nc.dram_tensor("scr16_lo", [D, LH], BF16)
    ffn16 = nc.dram_tensor("ffn16", [NEXP * 6, 128, 4096], BF16)
    scr_q = nc.dram_tensor("scr_q", [128, NLT * 4096], F32)
    scr_k = nc.dram_tensor("scr_k", [128, NLT * 4096], F32)
    scr_v = nc.dram_tensor("scr_v", [128, NLT * 4096], F32)
    scr_dot = nc.dram_tensor("scr_dot", [32, LH], F32)
    ag_in = nc.dram_tensor("ag_in", [BLOB], F32)
    ag_out = nc.dram_tensor("ag_out", [n_cc * BLOB], F32)
    ar_in = nc.dram_tensor("ar_in", [1, 16], F32)
    ar_out = nc.dram_tensor("ar_out", [1, 16], F32)

    es = contextlib.ExitStack()
    with es:
        AR_W = 53200
        arena = es.enter_context(nc.sbuf_tensor("arena", [128, AR_W], F32))
        off = [0]

        def alloc(cols, rows=128, row0=0):
            o = off[0]
            off[0] += cols
            assert off[0] <= AR_W, f"arena overflow {off[0]}"
            return View(arena, o, cols, rows, row0)

        psum = [es.enter_context(nc.psum_tensor(f"ps{i}", [128, 512], F32)) for i in range(8)]
        sems = {}
        for e in ENGS:
            sems[e] = es.enter_context(nc.semaphore("s_" + e))
            sems[e + "_dma"] = es.enter_context(nc.semaphore("sd_" + e))
        sems["cc"] = es.enter_context(nc.semaphore("s_cc"))

        c_id128 = alloc(128)
        c_mask = alloc(128)
        c_obd16 = alloc(128)
        c_selp16 = alloc(1024, rows=16)
        c_ones1 = alloc(128, rows=1)
        c_onescol = alloc(1)
        c_norm = alloc(LH, rows=48)
        c_bq = alloc(8); c_bk = alloc(8)
        c_lb = alloc(8, rows=1)
        c_selcar = alloc(n_cc); c_one = alloc(1)
        c_gwt = alloc(64)

        w_q = alloc(8192); w_k = alloc(8192); w_v = alloc(8192)
        arena16 = arena.bitcast(mybir.dt.bfloat16)

        def V16(word_off, cols16, rows=128, row0=0):
            return View(arena16, 2 * word_off, cols16, rows, row0)

        WR = w_q.off
        r_tm4 = View(arena, WR, 4096)
        lnb_hi = V16(WR + 4096, 1024)
        lnb_lo = V16(WR + 4608, 1024)
        stg = View(arena, WR + 5120, 4096)
        st_hi = V16(WR + 9216, 4096)
        st_lo = V16(WR + 11264, 4096)
        wo16_hi = V16(WR + 13312, 8192)
        wo16_lo = V16(WR + 17408, 8192)
        r_tm = View(arena, WR + 512, 1024)
        x_tm_sb = View(arena, WR + 1536, 1024)
        f_lnh = [V16(WR + 0, 4096), V16(WR + 4096, 4096)]
        f_lnl = [V16(WR + 2048, 4096), V16(WR + 6144, 4096)]
        f_wgh = V16(WR + 8192, 4096); f_wgl = V16(WR + 10240, 4096)
        f_wuh = V16(WR + 12288, 4096); f_wul = V16(WR + 14336, 4096)
        f_wdh = V16(WR + 16384, 4096); f_wdl = V16(WR + 18432, 4096)
        f_hh = V16(WR + 20480, 2048); f_hl = V16(WR + 21504, 2048)

        x_t = alloc(4096); q_t = alloc(4096); k_t = alloc(4096); v_t = alloc(4096)
        acc = View(arena, x_t.off, 16384)
        gwt16 = alloc(64)
        gwt_hi = View(arena16, 2 * gwt16.off, 64)
        gwt_lo = View(arena16, 2 * gwt16.off + 64, 64)

        sc0 = alloc(512); sc1 = alloc(512); pr0 = alloc(512); pr1 = alloc(512)
        ktm = alloc(256); sct = alloc(64)
        kv_glob = alloc(512); kv_car = alloc(512); kv_tmp = alloc(512)
        chain = alloc(4096, rows=16)
        f_si = View(arena, kv_glob.off, 512)
        f_hf = View(arena, kv_car.off, 512)
        f2_wgh = V16(chain.off, 4096); f2_wgl = V16(chain.off + 2048, 4096)
        f2_wuh = V16(sc0.off, 4096); f2_wul = V16(WR + 22528, 4096)
        slot8 = View(arena, WR + 4608, 4096)
        slotc = View(arena, WR + 8704, 256)
        slotcs = View(arena, WR + 8960, 8, rows=16)
        q16_hi = V16(q_t.off, 4096)
        q16_lo = V16(q_t.off + 2048, 4096)
        wq16 = V16(w_q.off, 16384)
        wk16 = V16(w_k.off, 16384)
        wv16w = V16(w_v.off, 16384)
        xt16 = V16(x_t.off, 8192)
        c_bv = View(arena, chain.off, 1024, rows=1, row0=32)
        c_bo = View(arena, chain.off + 1024, 1024, rows=1, row0=32)
        c_ones32 = View(arena, chain.off + 2048, 128, rows=1, row0=32)
        st_col = alloc(48); st_cs = alloc(2, rows=16)
        car_k = alloc(8); car_q = alloc(8); car_kso = alloc(8); car_qsi = alloc(8)
        car_cs = alloc(1, rows=16); blobtmp = alloc(8)
        ln_sml = alloc(20); g_sml = alloc(96)
        sel_all = alloc(128); c_all = alloc(128); snacc = alloc(16); wm_bc = alloc(8)

        r = Rec()
        pe_last = {}

        def slot(i, c0=0, c1=512):
            return chain[0:16, 512 * i + c0:512 * i + c1]

        def nrows(kind, c0, c1):
            base = 0 if kind == "n" else 32
            return c_norm[base:base + 16, c0:c1] if kind == "n" else arena[32:48, c_norm.off + c0:c_norm.off + c1]

        def load(dst, src):
            return r.dma_op("sync", lambda e, d=dst, s=src: e.dma_start(out=d, in_=s))

        load(c_id128.all(), dt["c_id128"].ap())
        load(c_mask.all(), dt["c_mask"].ap())
        load(c_obd16.all(), dt["c_obd16"].ap())
        load(c_selp16.all(), dt["c_selp16"].ap())
        load(c_ones1.all(), dt["c_ones1"].ap())
        load(c_onescol.all(), dt["c_onescol"].ap())
        load(c_norm.all(), dt["c_norm"].ap())
        load(c_bq.all(), dt["b_q"].ap())
        load(c_bk.all(), dt["b_k"].ap())
        load(c_bv.all(), dt["b_v"].ap())
        load(c_bo.all(), dt["b_o"].ap())
        load(c_ones32.all(), dt["c_ones1"].ap())
        load(c_lb.all(), dt["lb"].ap())
        load(c_selcar.all(), dt["c_selcar"].ap())
        load(c_one.all(), dt["c_one"].ap())
        for m in range(8):
            load(c_gwt[:, 8 * m:8 * m + 8], dt["gw_t"].ap()[128 * m:128 * (m + 1), :])
            for wn, wv in (("w_q", w_q), ("w_k", w_k), ("w_v", w_v)):
                load(wv[:, 1024 * m:1024 * (m + 1)], dt[wn].ap()[128 * m:128 * (m + 1), :])
        ftok = r.bump_after_dma("sync")
        r.wait("act", ftok)
        gt1 = r.tick("act", lambda e: e.copy(gwt_hi.all(), c_gwt.all()))
        r.wait("dve", gt1)
        r.tick("dve", lambda e: e.tensor_tensor(gwt_lo.all(), c_gwt.all(), gwt_hi.all(), OP.subtract))
        r.drain("dve")

        r.op("dve", lambda e: e.memset(snacc.all(), 0.0))
        r.op("dve", lambda e: e.memset(kv_car.all(), 0.0))
        r.op("dve", lambda e: e.memset(kv_glob.all(), 0.0))
        r.op("dve", lambda e: e.memset(st_col.all(), 0.0))
        r.op("dve", lambda e: e.memset(st_cs.all(), 0.0))
        r.op("dve", lambda e: e.memset(
            View(arena, c_norm.off, 512, rows=1, row0=64).all(), 0.0))
        r.drain("dve")
        wscr = View(arena, q_t.off, 1024)
        prev_sub = None
        for wv_ in (w_q, w_k, w_v):
            wv16b = V16(wv_.off, 16384)
            for m in range(8):
                r.wait("act", prev_sub)
                r.tick("act", lambda e, wv_=wv_, m=m: e.copy(
                    wscr.all(), wv_[:, 1024 * m:1024 * (m + 1)]))
                r.drain("act")
                htok = r.tick("act", lambda e, wv16b=wv16b, m=m: e.copy(
                    wv16b[:, 2048 * m:2048 * m + 1024], wscr.all()))
                r.wait("dve", htok)
                prev_sub = r.tick("dve", lambda e, wv16b=wv16b, m=m: e.tensor_tensor(
                    wv16b[:, 2048 * m + 1024:2048 * (m + 1)], wscr.all(),
                    wv16b[:, 2048 * m:2048 * m + 1024], OP.subtract))
        r.drain("dve")
        r.barrier()

        def scan(dst, src, init_ap, init_imm):
            if init_ap is None:
                return r.tick("dve", lambda e, d=dst, s=src, v=init_imm:
                              e.tensor_tensor_scan(d, s, s, v, OP.add, OP.bypass))
            return r.tick("dve", lambda e, d=dst, s=src, ia=init_ap:
                          e.tensor_tensor_scan(d, s, s, ia, OP.add, OP.bypass))

        def dotmm(bank, rows0, p, prod_ap, start, stop):
            tp = None if rows0 == 0 else (0, rows0)
            r.op("pe", lambda e, bank=bank, rows0=rows0, p=p, pa=prod_ap, start=start, stop=stop, tp=tp:
                 e.matmul(psum[bank][rows0:rows0 + 16, 0:512],
                          c_obd16[:, 16 * p:16 * (p + 1)], pa,
                          start=start, stop=stop, tile_position=tp))

        z16l = V16(c_norm.off, 128, rows=1, row0=64)
        z16r = V16(c_norm.off, 1024, rows=1, row0=64)

        def grp(bank, open_, cols=512, rows=128):
            r.op("pe", lambda e, bank=bank, open_=open_, cols=cols, rows=rows: e.matmul(
                psum[bank][0:rows, 0:cols], z16l[0:1, 0:128], z16r[0:1, 0:cols],
                start=open_, stop=not open_))

        def bc16(bank, src16, p, cols=512, o=0):
            r.op("pe", lambda e, bank=bank, s=src16, p=p, cols=cols, o=o:
                 e.matmul(psum[bank][:, o:o + cols], c_selp16[0:16, 128 * p:128 * (p + 1)], s,
                          start=True, stop=True))

        split_state = {"sub": None, "spill": None}
        vs_state = {"tok": None}
        hoist = {}
        kv_state = {}
        wo_state = {"ready": None}

        def wo_split():
            for half in range(2):
                r.wait("pool", split_state["sub"])
                r.wait("pool", split_state["spill"])
                for m in range(4):
                    r.dma_op("pool", lambda e, half=half, m=m: e.dma_start(
                        out=stg[:, 1024 * m:1024 * (m + 1)],
                        in_=dt["w_o"].ap()[512 * half + 128 * m:512 * half + 128 * (m + 1), :]))
                r.bump_after_dma("pool")
                htok = r.tick("pool", lambda e, half=half: e.tensor_copy(
                    wo16_hi[:, 4096 * half:4096 * (half + 1)], stg.all()))
                r.drain("pool")
                gtok = r.tick("pool", lambda e, half=half: e.tensor_sub(
                    wo16_lo[:, 4096 * half:4096 * (half + 1)], stg.all(),
                    wo16_hi[:, 4096 * half:4096 * (half + 1)]))
                split_state["sub"] = gtok
                split_state["spill"] = htok
            wo_state["ready"] = split_state["sub"]

        def ffn_split_batch(mats):
            if "split" in SKIP:
                return
            for i in mats:
                e_i, kind = divmod(i, 3)
                src = (dt["ffn_gw"], dt["ffn_uw"], dt["ffn_dw"])[kind]
                r.wait("pool", split_state["sub"])
                r.wait("pool", split_state["spill"])
                if kind < 2:
                    for m in range(8):
                        r.dma_op("pool", lambda e, src=src, e_i=e_i, m=m: e.dma_start(
                            out=stg[:, 512 * m:512 * (m + 1)],
                            in_=src.ap()[e_i, 128 * m:128 * (m + 1), :]))
                else:
                    for i4 in range(4):
                        r.dma_op("pool", lambda e, src=src, e_i=e_i, i4=i4: e.dma_start(
                            out=stg[:, 1024 * i4:1024 * (i4 + 1)],
                            in_=src.ap()[e_i, 128 * i4:128 * (i4 + 1), :]))
                r.bump_after_dma("pool")
                htok = r.tick("pool", lambda e: e.tensor_copy(st_hi.all(), stg.all()))
                r.wait("pool", htok)
                gtok = r.tick("pool", lambda e: e.tensor_sub(st_lo.all(), stg.all(), st_hi.all()))
                split_state["sub"] = gtok
                r.wait("pool", gtok)
                idx = 2 * i
                r.dma_op("pool", lambda e, idx=idx: e.dma_start(
                    out=ffn16.ap()[idx, :, :], in_=st_hi.all()))
                r.dma_op("pool", lambda e, idx=idx: e.dma_start(
                    out=ffn16.ap()[idx + 1, :, :], in_=st_lo.all()))
                split_state["spill"] = r.bump_after_dma("pool")

        def attention_pass(phase):
            isB = phase == "B"

            if isB:
                r.op("dve", lambda e: e.tensor_scalar_add(st_col[:, 32:40], car_kso.all(), EPS))
                r.op("dve", lambda e: e.tensor_scalar_add(st_col[:, 40:48], car_qsi.all(), EPS))
                r.op("dve", lambda e: e.tensor_copy(st_cs[0:16, 1:2], car_cs.all()))
                r.drain("dve")
                r.tick("dve", lambda e: e.nop())
                r.barrier()
                r.wait("sync", kv_state["cc2"])
                for q in range(n_cc):
                    r.dma_op("sync", lambda e, q=q: e.dma_start(
                        out=r_tm4[:, 512 * q:512 * (q + 1)],
                        in_=ag_out.ap()[65536 * q:65536 * (q + 1)]
                        .rearrange("(p c) -> p c", p=128)))
                kv_state["dma"] = r.bump_after_dma("sync")

            xsp_state = {}

            def x_split(jj):
                for m in range(8):
                    r.dma_op("sync", lambda e, m=m, jj=jj: e.dma_start(
                        out=x_t[:, 512 * m:512 * (m + 1)],
                        in_=dt["x_fm"].ap()[128 * m:128 * (m + 1), LT * jj:LT * (jj + 1)]))
                xltok = r.bump_after_dma("sync")
                dvetok = r.dve_tok()
                r.wait("act", xltok)
                r.wait("act", dvetok)
                sprev = {}
                for m in range(8):
                    scb = sc0 if m % 2 == 0 else sc1
                    r.wait("act", sprev.get(m % 2))
                    r.tick("act", lambda e, m=m, scb=scb: e.copy(
                        scb.all(), x_t[:, 512 * m:512 * (m + 1)]))
                    r.drain("act")
                    htok = r.tick("act", lambda e, m=m, scb=scb: e.copy(
                        xt16[:, 1024 * m:1024 * m + 512], scb.all()))
                    r.wait("dve", htok)
                    sprev[m % 2] = r.tick("dve", lambda e, m=m, scb=scb: e.tensor_tensor(
                        xt16[:, 1024 * m + 512:1024 * (m + 1)], scb.all(),
                        xt16[:, 1024 * m:1024 * m + 512], OP.subtract))
                r.drain("dve")
                xsp_state[jj] = r.tick("dve", lambda e: e.nop())

            if not isB:
                x_split(0)
            for j in range(NLT):
                if isB:
                    if j == 0:
                        wo_split()
                    ffn_split_batch(range(8 * j, min(8 * j + 8, 3 * NEXP)))
                if not isB:
                    xsp = xsp_state.pop(j)
                    r.wait("pe", xsp)
                    qk_etok = {}
                    for f in range(8):
                        for wi, (w16, dst, bias) in enumerate(((wq16, q_t, c_bq), (wk16, k_t, c_bk))):
                            bk = 2 * wi + f % 2
                            r.wait("pe", pe_last.get(bk))
                            for pas in range(3):
                                wo_, xo_ = ((0, 0), (0, 512), (1024, 0))[pas]
                                for m in range(8):
                                    r.op("pe", lambda e, w16=w16, m=m, f=f, bk=bk,
                                         wo_=wo_, xo_=xo_, pas=pas: e.matmul(
                                        psum[bk][:, 0:512],
                                        w16[:, 2048 * m + wo_ + 128 * f:
                                            2048 * m + wo_ + 128 * (f + 1)],
                                        xt16[:, 1024 * m + xo_:1024 * m + xo_ + 512],
                                        start=(pas == 0 and m == 0),
                                        stop=(pas == 2 and m == 7)))
                            ptok = r.tick_last("pe")
                            r.wait("act", ptok)
                            etok = r.tick("act", lambda e, bk=bk, dst=dst, f=f, bias=bias:
                                          e.activation(dst[:, 512 * f:512 * (f + 1)],
                                                       psum[bk][:, 0:512], AF.Sigmoid,
                                                       bias=bias[:, f:f + 1], scale=1.0))
                            pe_last[bk] = etok
                            qk_etok[(wi, f)] = etok
                            r.wait("sync", etok)
                            r.dma_op("sync", lambda e, dst=dst, f=f, j=j, wi=wi: e.dma_start(
                                out=(scr_q if wi == 0 else scr_k).ap()[
                                    :, 4096 * j + 512 * f:4096 * j + 512 * (f + 1)],
                                in_=dst[:, 512 * f:512 * (f + 1)]))
                    def v_mm(tbs):
                        ptokv = {}
                        for tb in tbs:
                            for hf in range(2):
                                bk = (2 * tb + hf) % 4
                                r.wait("pe", pe_last.get(bk))
                                for pas in range(3):
                                    xo_, wo_ = ((0, 0), (512, 0), (0, 1024))[pas]
                                    for m in range(8):
                                        r.op("pe", lambda e, m=m, tb=tb, hf=hf, bk=bk,
                                             xo_=xo_, wo_=wo_, pas=pas: e.matmul(
                                            psum[bk][:, 0:512],
                                            xt16[:, 1024 * m + xo_ + 128 * tb:
                                                 1024 * m + xo_ + 128 * (tb + 1)],
                                            wv16w[:, 2048 * m + wo_ + 512 * hf:
                                                  2048 * m + wo_ + 512 * (hf + 1)],
                                            start=(pas == 0 and m == 0), stop=False))
                                r.op("pe", lambda e, hf=hf, bk=bk: e.matmul(
                                    psum[bk][:, 0:512], c_ones32.all(),
                                    c_bv[0:1, 512 * hf:512 * (hf + 1)], start=False, stop=True))
                                ptokv[(tb, hf)] = r.tick_last("pe")
                        return ptokv

                    def v_evac(tbs, ptokv):
                        for tb in tbs:
                            for hf in range(2):
                                bk = (2 * tb + hf) % 4
                                r.wait("dve", ptokv[(tb, hf)])
                                etok = r.tick("dve", lambda e, bk=bk, tb=tb, hf=hf: e.tensor_copy(
                                    v_t[:, 1024 * tb + 512 * hf:1024 * tb + 512 * (hf + 1)],
                                    psum[bk][:, 0:512]))
                                pe_last[bk] = etok
                                r.wait("sync", etok)
                                r.dma_op("sync", lambda e, tb=tb, hf=hf, j=j: e.dma_start(
                                    out=scr_v.ap()[:, 4096 * j + 1024 * tb + 512 * hf:
                                                   4096 * j + 1024 * tb + 512 * (hf + 1)],
                                    in_=v_t[:, 1024 * tb + 512 * hf:
                                            1024 * tb + 512 * (hf + 1)]))

                    pt1 = v_mm((0, 1))
                    v_evac((0, 1), pt1)
                    pt2 = v_mm((2, 3))
                    s1adot = {}
                    for p in range(NPAIR):
                        qv = q_t[:, 512 * p:512 * (p + 1)]
                        kvw = k_t[:, 512 * p:512 * (p + 1)]
                        r.wait("dve", qk_etok[(0, p)])
                        r.wait("dve", qk_etok[(1, p)])
                        for si, (src, other, dbank, scb, so_) in enumerate(
                                ((kvw, qv, 6, sc0, 0), (qv, kvw, 5, sc1, 8))):
                            prb = (pr0, kv_tmp, pr1, kv_car)[2 * si + p % 2]
                            r.wait("dve", s1adot.get((p - 2, si)))
                            scan(scb.all(), src,
                                 None if j == 0 else st_col[:, so_ + p:so_ + p + 1], EPS)
                            r.drain("dve")
                            r.op("dve", lambda e, scb=scb, so_=so_, p=p: e.tensor_copy(
                                st_col[:, so_ + p:so_ + p + 1], scb[:, 511:512]))
                            ptk = r.tick("dve", lambda e, prb=prb, o=other, scb=scb:
                                         e.scalar_tensor_tensor(prb.all(), o, EPS, scb.all(),
                                                                OP.add, OP.mult))
                            r.drain("dve")
                            r.wait("pe", ptk)
                            if p == 0:
                                r.wait("pe", pe_last.get(dbank))
                            dotmm(dbank, 0, p, prb.all(),
                                  start=(p == 0), stop=(p == NPAIR - 1))
                            s1adot[(p, si)] = r.tick_last("pe")
                    v_evac((2, 3), pt2)
                    sb = r.bump_after_dma("sync")
                    r.wait("dve", sb)
                else:
                    if j == 0 and hoist.get("rl0") is not None:
                        ftok = hoist["rl0"]
                    else:
                        for src, dst in ((scr_q, q_t), (scr_k, k_t), (scr_v, v_t)):
                            r.dma_op("sync", lambda e, src=src, dst=dst, j=j: e.dma_start(
                                out=dst.all(), in_=src.ap()[:, 4096 * j:4096 * (j + 1)]))
                        ftok = r.bump_after_dma("sync")
                    r.wait("dve", ftok)
                    r.wait("pe", ftok)

                s1dot = {}
                for p in range(NPAIR if isB else 0):
                    qv = q_t[:, 512 * p:512 * (p + 1)]
                    kvw = k_t[:, 512 * p:512 * (p + 1)]
                    if True:
                        for si, (src, car, dbank) in enumerate(((qv, car_k, 6), (kvw, car_q, 5))):
                            prb = (pr0, pr1, sc0, sc1)[2 * (p % 2) + si]
                            r.wait("dve", s1dot.get((p - 2, si)))
                            ptk = r.tick("dve", lambda e, src=src, car=car, p=p, prb=prb:
                                         e.tensor_scalar(prb.all(), src, EPS, car[:, p:p + 1],
                                                         OP.add, OP.mult))
                            r.drain("dve")
                            r.wait("pe", ptk)
                            if p == 0:
                                r.wait("pe", pe_last.get(dbank))
                            dotmm(dbank, 0, p, prb.all(),
                                  start=(p == 0), stop=(p == NPAIR - 1))
                            s1dot[(p, si)] = r.tick_last("pe")
                ptok = r.pe_tok()
                r.wait("dve", ptok)
                if not isB:
                    r.tick("dve", lambda e: e.tensor_copy(slot(0), psum[6][0:16, 0:512]))
                    r.tick("dve", lambda e: e.tensor_copy(slot(1), psum[5][0:16, 0:512]))
                    pe_last[6] = r.dve_tok()
                    pe_last[5] = r.dve_tok()
                    r.drain("dve")
                    dtk = r.dve_tok()
                    r.wait("sync", dtk)
                    r.dma_op("sync", lambda e, j=j: e.dma_start(
                        out=scr_dot.ap()[0:16, LT * j:LT * (j + 1)], in_=slot(0)))
                    r.dma_op("sync", lambda e, j=j: e.dma_start(
                        out=scr_dot.ap()[16:32, LT * j:LT * (j + 1)], in_=slot(1)))
                    den_q, den_k = slot(0), slot(1)
                else:
                    r.dma_op("sync", lambda e, j=j: e.dma_start(
                        out=slot(0), in_=scr_dot.ap()[0:16, LT * j:LT * (j + 1)]))
                    r.dma_op("sync", lambda e, j=j: e.dma_start(
                        out=slot(1), in_=scr_dot.ap()[16:32, LT * j:LT * (j + 1)]))
                    r.await_dma("dve", "sync")
                    r.tick("dve", lambda e: e.tensor_tensor(slot(5), slot(0), psum[6][0:16, 0:512], OP.add))
                    r.tick("dve", lambda e: e.tensor_tensor(slot(6), slot(1), psum[5][0:16, 0:512], OP.add))
                    r.drain("dve")
                    den_q, den_k = slot(5), slot(6)
                pe_last[6] = r.dve_tok()
                pe_last[5] = r.dve_tok()
                r.op("dve", lambda e, d=den_q: e.reciprocal(slot(2), d))
                r.op("dve", lambda e, d=den_k: e.reciprocal(slot(7), d))
                r.drain("dve")
                r.op("dve", lambda e, j=j: e.tensor_tensor(
                    slot(3), slot(2), nrows("n", LT * j, LT * (j + 1)), OP.mult))
                r.tick("dve", lambda e, j=j: e.tensor_tensor(
                    slot(4), slot(7), nrows("n", LT * j, LT * (j + 1)), OP.mult))
                r.drain("dve")

                s3tok = {}
                s3dot = {}
                s5tok = {}
                s5dot = {}
                spe0 = r.pe_tok()

                def s3_bc(p):
                    bA = 5 if p % 2 == 0 else 0
                    r.wait("pe", pe_last.get(bA))
                    bc16(bA, slot(4), p)
                    s3tok[p] = r.tick_last("pe")

                def s5_bc(p):
                    bA = 1 if p % 2 == 0 else 3
                    r.wait("pe", pe_last.get(bA))
                    bc16(bA, slot(3), p)
                    if isB:
                        bB = 2 if p % 2 == 0 else 7
                        r.wait("pe", pe_last.get(bB))
                        bc16(bB, slot(2), p)
                    s5tok[p] = r.tick_last("pe")

                r.wait("pe", r.dve_tok())
                s3_bc(0)
                s5_bc(0)
                for p in range(NPAIR):
                    b3 = 5 if p % 2 == 0 else 0
                    b5 = 1 if p % 2 == 0 else 3
                    bB = 2 if p % 2 == 0 else 7
                    kvw = k_t[:, 512 * p:512 * (p + 1)]
                    qv = q_t[:, 512 * p:512 * (p + 1)]
                    pr3 = pr0 if p % 2 == 0 else pr1
                    pr5 = kv_tmp if p % 2 == 0 else kv_car
                    r.wait("dve", s3tok[p])
                    r.wait("dve", s3dot.get(p - 2, spe0 if p < 2 else None))
                    sot = r.tick("dve", lambda e, kvw=kvw, b3=b3, pr3=pr3: e.tensor_tensor(
                        pr3.all(), kvw, psum[b3][:, 0:512], OP.mult))
                    pe_last[b3] = sot
                    if p + 1 < NPAIR:
                        s3_bc(p + 1)
                    r.drain("dve")
                    scan(sc0.all(), pr3.all(),
                         st_col[:, 32 + p:33 + p] if (isB and j == 0)
                         else (None if j == 0 else st_col[:, 16 + p:17 + p]), EPS)
                    r.drain("dve")
                    r.op("dve", lambda e, p=p: e.tensor_copy(st_col[:, 16 + p:17 + p], sc0[:, 511:512]))
                    if isB:
                        ptk3 = r.tick("dve", lambda e, qv=qv, pr3=pr3: e.scalar_tensor_tensor(
                            pr3.all(), qv, EPS, sc0.all(), OP.add, OP.mult))
                        r.drain("dve")
                        r.wait("pe", ptk3)
                        if p == 0:
                            r.wait("pe", pe_last.get(6))
                        dotmm(6, 0, p, pr3.all(), start=(p == 0), stop=(p == NPAIR - 1))
                        s3dot[p] = r.tick_last("pe")
                    r.wait("dve", s5tok[p])
                    r.wait("dve", s5dot.get(p - 2, spe0 if p < 2 else None))
                    qsit = r.tick("dve", lambda e, qv=qv, b5=b5, pr5=pr5: e.tensor_tensor(
                        pr5.all(), qv, psum[b5][:, 0:512], OP.mult))
                    pe_last[b5] = qsit
                    if isB:
                        qst = r.tick("dve", lambda e, qv=qv, p=p, bB=bB: e.tensor_tensor(
                            x_t[:, 512 * p:512 * (p + 1)], qv, psum[bB][:, 0:512], OP.mult))
                        pe_last[bB] = qst
                    if p + 1 < NPAIR:
                        s5_bc(p + 1)
                    r.drain("dve")
                    scan(sc1.all(), pr5.all(),
                         st_col[:, 40 + p:41 + p] if (isB and j == 0)
                         else (None if j == 0 else st_col[:, 24 + p:25 + p]), EPS)
                    r.drain("dve")
                    r.op("dve", lambda e, p=p: e.tensor_copy(st_col[:, 24 + p:25 + p], sc1[:, 511:512]))
                    ptk5 = r.tick("dve", lambda e, kvw=kvw, pr5=pr5: e.scalar_tensor_tensor(
                        pr5.all(), kvw, EPS, sc1.all(), OP.add, OP.mult))
                    r.drain("dve")
                    r.wait("pe", ptk5)
                    if p == 0:
                        r.wait("pe", pe_last.get(4))
                    dotmm(4, 0, p, pr5.all(), start=(p == 0), stop=(p == NPAIR - 1))
                    s5dot[p] = r.tick_last("pe")
                if isB:
                    ptok = r.pe_tok()
                    r.wait("dve", ptok)
                    r.tick("dve", lambda e, j=j: e.tensor_tensor(
                        slot(5), psum[6][0:16, 0:512], nrows("r", LT * j, LT * (j + 1)), OP.mult))
                    pe_last[6] = r.dve_tok()
                    r.drain("dve")
                    dtk = r.dve_tok()
                    r.wait("act", dtk)
                    atok = r.tick("act", lambda e: e.activation(slot(7), slot(5), AF.Sigmoid))
                    r.wait("dve", atok)
                ptok = r.pe_tok()
                r.wait("dve", ptok)
                r.tick("dve", lambda e, j=j: e.tensor_tensor(
                    slot(6), psum[4][0:16, 0:512], nrows("r", LT * j, LT * (j + 1)), OP.mult))
                pe_last[4] = r.dve_tok()
                r.drain("dve")
                dtk = r.dve_tok()
                r.wait("act", dtk)
                atok = r.tick("act", lambda e: e.activation(slot(5), slot(6), AF.Exp))
                r.wait("dve", atok)
                scan(slot(6), slot(5),
                     st_cs[0:16, 1:2] if (isB and j == 0)
                     else (None if j == 0 else st_cs[0:16, 0:1]), 0.0)
                r.drain("dve")
                r.op("dve", lambda e: e.tensor_copy(
                    st_cs[0:16, 0:1], chain[0:16, 512 * 6 + 511:512 * 6 + 512]))
                r.drain("dve")
                r.op("dve", lambda e: e.reciprocal(slot(2), slot(6)))
                r.drain("dve")
                r.op("dve", lambda e: e.tensor_tensor(slot(3), slot(2), slot(5), OP.mult))
                r.drain("dve")
                r.tick("dve", lambda e, j=j: e.tensor_tensor(
                    slot(5), slot(3), nrows("n", LT * j, LT * (j + 1)), OP.mult))
                r.drain("dve")
                def sct_t(c):
                    bk = 5 if c % 2 == 0 else 0
                    r.wait("pe", pe_last.get(bk))
                    r.op("pe", lambda e, c=c, bk=bk: e.transpose(
                        psum[bk][0:128, 0:16], slot(5, 128 * c, 128 * (c + 1)),
                        c_id128[0:16, 0:16]))
                    return r.tick_last("pe"), bk

                r.wait("pe", r.dve_tok())
                tts = {0: sct_t(0)}
                for c in range(NCHT):
                    ptk, bk = tts[c]
                    r.wait("dve", ptk)
                    etok = r.tick("dve", lambda e, c=c, bk=bk: e.tensor_copy(
                        sct[:, 16 * c:16 * (c + 1)], psum[bk][0:128, 0:16]))
                    pe_last[bk] = etok
                    if c + 1 < NCHT:
                        tts[c + 1] = sct_t(c + 1)
                    r.wait("act", etok)
                    for h in range(16):
                        tick = (h == 15)
                        fn = lambda e, c=c, h=h: e.mul(
                            v_t[:, 1024 * c + 64 * h:1024 * c + 64 * (h + 1)],
                            v_t[:, 1024 * c + 64 * h:1024 * c + 64 * (h + 1)],
                            sct[:, 16 * c + h:16 * c + h + 1])
                        if tick:
                            vstok = r.tick("act", fn)
                            vs_state[c] = vstok
                            vs_state["tok"] = vstok
                        else:
                            r.op("act", fn)

                if not isB:
                    if j == 0:
                        r.wait("pe", pe_last.get(7))
                        grp(7, True)
                    kvseq = [(c, p) for c in range(NCHT) for p in range(NPAIR)]
                    ev = {}

                    def kv_t(i):
                        c, p = kvseq[i]
                        bk = 5 if i % 2 == 0 else 0
                        r.wait("pe", ev.get(i - 2, pe_last.get(bk) if i < 2 else None))
                        if i < 2 or kvseq[i - 1][0] != c:
                            r.wait("pe", vs_state[c])
                        r.op("pe", lambda e, p=p, c=c, bk=bk: e.transpose(
                            psum[bk][:, 0:128],
                            k_t[:, 512 * p + 128 * c:512 * p + 128 * (c + 1)], c_id128.all()))
                        return r.tick_last("pe")

                    ttok = {0: kv_t(0)}
                    for i in range(len(kvseq)):
                        c, p = kvseq[i]
                        par = i % 2
                        bk = 5 if par == 0 else 0
                        r.wait("dve", ttok[i])
                        ev[i] = r.tick("dve", lambda e, par=par, bk=bk: e.tensor_copy(
                            ktm[:, 128 * par:128 * (par + 1)], psum[bk][:, 0:128]))
                        if i + 1 < len(kvseq):
                            ttok[i + 1] = kv_t(i + 1)
                        r.wait("pe", ev[i])
                        for hh in range(2):
                            h = 2 * p + hh
                            r.op("pe", lambda e, p=p, c=c, h=h, hh=hh, par=par: e.matmul(
                                psum[7][64 * hh:64 * (hh + 1), 64 * p:64 * (p + 1)],
                                ktm[:, 128 * par + 64 * hh:128 * par + 64 * (hh + 1)],
                                v_t[:, 1024 * c + 64 * h:1024 * c + 64 * (h + 1)],
                                start=False, stop=False,
                                tile_position=(0, 64) if hh else None))
                        r.tick_last("pe")
                    pe_last[5] = ev[len(kvseq) - 1]
                    pe_last[0] = ev[len(kvseq) - 1]
                    if j + 1 < NLT:
                        x_split(j + 1)
                else:
                    if j == 0:
                        r.wait("dve", kv_state["dma"])
                        r.op("dve", lambda e: e.memset(kv_car.all(), 0.0))
                        r.drain("dve")
                        for q in range(n_cc):
                            r.op("dve", lambda e, q=q: e.scalar_tensor_tensor(
                                kv_car.all(), r_tm4[:, 512 * q:512 * (q + 1)],
                                c_selcar[:, q:q + 1], kv_car.all(), OP.mult, OP.add))
                            r.drain("dve")
                        kv_state["tok"] = r.tick("dve", lambda e: e.tensor_copy(
                            kv_glob.all(), kv_car.all()))
                        r.drain("dve")
                    mixer_ltile(j)
                r.barrier()

            if not isB:
                grp(7, False)
                ptok = r.tick_last("pe")
                r.wait("dve", ptok)
                r.tick("dve", lambda e: e.tensor_copy(kv_tmp.all(), psum[7][:, 0:512]))
                pe_last[7] = r.dve_tok()
                for so_, car in ((0, car_k), (8, car_q), (16, car_kso), (24, car_qsi)):
                    r.drain("dve")
                    r.op("dve", lambda e, so_=so_: e.tensor_scalar_add(
                        blobtmp.all(), st_col[:, so_:so_ + 8], -EPS))
                    r.drain("dve")
                    r.op("dve", lambda e, car=car: e.tensor_copy(car.all(), blobtmp.all()))
                r.drain("dve")
                r.op("dve", lambda e: e.tensor_copy(car_cs.all(), st_cs[0:16, 0:1]))
                r.drain("dve")
                r.tick("dve", lambda e: e.nop())
                r.barrier()

        def mixer_ltile(j):
            st1 = {}
            pa_last = {}

            def stage1(c, p):
                par = p % 2
                sA, sB = (6, 1) if par == 0 else (2, 0)
                ob = 7 if par == 0 else 3
                r.wait("pe", pe_last.get(sA))
                r.wait("pe", pe_last.get(sB))
                r.wait("pe", pa_last.get(sB))
                r.op("pe", lambda e, p=p, c=c, sB=sB: e.transpose(
                    psum[sB][:, 128:256],
                    k_t[:, 512 * p + 128 * c:512 * p + 128 * (c + 1)], c_id128.all()))
                for hh in range(2):
                    r.op("pe", lambda e, p=p, c=c, hh=hh, sA=sA, sB=sB: e.matmul(
                        psum[sA if hh == 0 else sB][:, 0:128],
                        k_t[64 * hh:64 * (hh + 1), 512 * p + 128 * c:512 * p + 128 * (c + 1)],
                        x_t[64 * hh:64 * (hh + 1), 512 * p + 128 * c:512 * p + 128 * (c + 1)],
                        start=True, stop=True))
                sptok = r.tick_last("pe")
                obguard = pe_last.get(ob)
                r.wait("pe", obguard)
                r.wait("pe", pa_last.get(ob))
                grp(ob, True, cols=128)
                for hh in range(2):
                    r.op("pe", lambda e, p=p, c=c, hh=hh, ob=ob: e.matmul(
                        psum[ob][64 * hh:64 * (hh + 1), 0:128],
                        kv_glob[64 * hh:64 * (hh + 1), 64 * p:64 * (p + 1)],
                        x_t[64 * hh:64 * (hh + 1), 512 * p + 128 * c:512 * p + 128 * (c + 1)],
                        start=False, stop=False,
                        tile_position=(64, 64) if hh else None))
                r.wait("act", sptok)
                ktok = r.tick("act", lambda e, par=par, sB=sB: e.copy(
                    ktm[:, 128 * par:128 * (par + 1)], psum[sB][:, 128:256]))
                pa_last[sB] = ktok
                r.wait("dve", sptok)
                for hh in range(2):
                    r.op("dve", lambda e, hh=hh, par=par, sA=sA, sB=sB: e.tensor_tensor(
                        pr0[:, 256 * par + 128 * hh:256 * par + 128 * (hh + 1)],
                        psum[sA if hh == 0 else sB][:, 0:128], c_mask.all(), OP.mult))
                r.drain("dve")
                smtok = r.tick("dve", lambda e: e.nop())
                pe_last[sA] = smtok
                pe_last[sB] = smtok
                st1[p] = (smtok, ktok, ob, par, obguard)

            def stage2(c, p):
                smtok, ktok, ob, par, obguard = st1[p]
                r.wait("pe", smtok)
                for hh in range(2):
                    r.op("pe", lambda e, p=p, c=c, hh=hh, ob=ob, par=par: e.matmul(
                        psum[ob][64 * hh:64 * (hh + 1), 0:128],
                        v_t[:, 1024 * c + 128 * p + 64 * hh:1024 * c + 128 * p + 64 * (hh + 1)],
                        pr0[:, 256 * par + 128 * hh:256 * par + 128 * (hh + 1)],
                        start=False, stop=False,
                        tile_position=(0, 64) if hh else None))
                grp(ob, False, cols=128)
                otok = r.tick_last("pe")
                r.wait("pe", ktok)
                for hh in range(2):
                    r.op("pe", lambda e, p=p, c=c, hh=hh, par=par: e.matmul(
                        psum[4][64 * hh:64 * (hh + 1), 64 * p:64 * (p + 1)],
                        ktm[:, 128 * par + 64 * hh:128 * par + 64 * (hh + 1)],
                        v_t[:, 1024 * c + 128 * p + 64 * hh:1024 * c + 128 * p + 64 * (hh + 1)],
                        start=False, stop=False,
                        tile_position=(0, 64) if hh else None))
                bc16(ob, chain[0:16, 512 * 7 + 128 * c:512 * 7 + 128 * (c + 1)], p,
                     cols=128, o=128)
                btok = r.tick_last("pe")
                r.wait("act", btok)
                r.wait("act", obguard)
                sbtok = r.tick("act", lambda e, par=par, ob=ob: e.copy(
                    pr1[:, 128 * par:128 * (par + 1)], psum[ob][:, 128:256]))
                pa_last[ob] = sbtok
                r.wait("dve", sbtok)
                r.wait("dve", otok)
                hotok = r.tick("dve", lambda e, p=p, c=c, par=par, ob=ob: e.tensor_tensor(
                    k_t[:, 512 * p + 128 * c:512 * p + 128 * (c + 1)],
                    psum[ob][:, 0:128], pr1[:, 128 * par:128 * (par + 1)], OP.mult))
                pe_last[ob] = hotok

            if j == 0:
                r.wait("pe", kv_state["tok"])
                r.wait("dve", kv_state["tok"])
            for c in range(NCHT):
                r.wait("pe", vs_state[c])
                r.wait("pe", pe_last.get(4))
                grp(4, True)
                for idx in range(NPAIR + 1):
                    if idx < NPAIR:
                        stage1(c, idx)
                    if idx > 0:
                        stage2(c, idx - 1)
                grp(4, False)
                cltok = r.tick_last("pe")
                r.wait("dve", cltok)
                r.tick("dve", lambda e: e.tensor_tensor(kv_tmp.all(), psum[4][:, 0:512],
                                                        kv_glob.all(), OP.add))
                r.drain("dve")
                r.op("dve", lambda e: e.tensor_copy(kv_glob.all(), kv_tmp.all()))
                r.drain("dve")
                gtok = r.tick("dve", lambda e: e.nop())
                pe_last[4] = gtok
            pe_last[5] = r.dve_tok()

            mg_pe = r.pe_tok()
            mg_dve = r.dve_tok()
            r.wait("sync", mg_pe)
            r.wait("sync", mg_dve)
            for tb in range(4):
                r.dma_op("sync", lambda e, tb=tb, j=j: e.dma_start(
                    out=r_tm4[:, 1024 * tb:1024 * (tb + 1)],
                    in_=dt["x_tm"].ap()[128 * (4 * j + tb):128 * (4 * j + tb + 1), :]))
            xttok = r.bump_after_dma("sync")
            r.wait("act", mg_pe)
            r.wait("act", mg_dve)
            for b in range(2):
                atok = r.tick("act", lambda e, b=b: e.copy(
                    q16_hi[:, 2048 * b:2048 * (b + 1)], k_t[:, 2048 * b:2048 * (b + 1)]))
                r.wait("dve", atok)
                r.op("dve", lambda e, b=b: e.tensor_tensor(
                    q16_lo[:, 2048 * b:2048 * (b + 1)], k_t[:, 2048 * b:2048 * (b + 1)],
                    q16_hi[:, 2048 * b:2048 * (b + 1)], OP.subtract))
            r.drain("dve")
            k16tok = r.tick("dve", lambda e: e.nop())
            r.wait("pe", k16tok)
            r.wait("pe", wo_state["ready"])
            for bk in range(8):
                r.wait("pe", pe_last.get(bk))
            for pas in range(3):
                lv, wv = ((q16_hi, wo16_hi), (q16_lo, wo16_hi), (q16_hi, wo16_lo))[pas]
                for d in range(8):
                    for tb in range(4):
                        for nh in range(2):
                            r.op("pe", lambda e, lv=lv, wv=wv, d=d, tb=tb, nh=nh,
                                 pas=pas: e.matmul(
                                psum[2 * tb + nh][:, 0:512],
                                lv[:, 512 * d + 128 * tb:512 * d + 128 * (tb + 1)],
                                wv[:, 1024 * d + 512 * nh:1024 * d + 512 * (nh + 1)],
                                start=(pas == 0 and d == 0), stop=False))
            for tb in range(4):
                for nh in range(2):
                    r.op("pe", lambda e, nh=nh, tb=tb: e.matmul(
                        psum[2 * tb + nh][:, 0:512], c_ones32.all(),
                        c_bo[0:1, 512 * nh:512 * (nh + 1)], start=False, stop=True))
            ptok = r.tick_last("pe")
            r.wait("dve", xttok)
            r.wait("dve", ptok)
            for tb in range(4):
                for nh in range(2):
                    etok = r.tick("dve", lambda e, tb=tb, nh=nh: e.tensor_tensor(
                        r_tm4[:, 1024 * tb + 512 * nh:1024 * tb + 512 * (nh + 1)],
                        psum[2 * tb + nh][:, 0:512],
                        r_tm4[:, 1024 * tb + 512 * nh:1024 * tb + 512 * (nh + 1)], OP.add))
                    pe_last[2 * tb + nh] = etok
            r.drain("dve")
            for tb in range(4):
                ln_norm(4 * j + tb, tb)

        def ln_norm(tbg, tb):
            rt = View(arena, WR + 1024 * tb, 1024)
            r.op("dve", lambda e, rt=rt: e.bn_stats(ln_sml[:, 0:6], rt[:, 0:512]))
            r.op("dve", lambda e, rt=rt: e.bn_stats(ln_sml[:, 6:12], rt[:, 512:1024]))
            r.drain("dve")
            r.op("dve", lambda e: e.bn_aggr(ln_sml[:, 12:14], ln_sml[:, 0:12]))
            r.drain("dve")
            r.op("dve", lambda e: e.tensor_scalar_add(ln_sml[:, 14:15], ln_sml[:, 13:14], 1e-5))
            r.drain("dve")
            stok = r.tick("dve", lambda e: e.nop())
            r.wait("act", stok)
            atok = r.tick("act", lambda e: e.activation(ln_sml[:, 15:16], ln_sml[:, 14:15], AF.Sqrt))
            r.wait("dve", atok)
            r.op("dve", lambda e: e.reciprocal(ln_sml[:, 16:17], ln_sml[:, 15:16]))
            r.drain("dve")
            lntok = r.tick("dve", lambda e, rt=rt: e.tensor_scalar(
                rt.all(), rt.all(), ln_sml[:, 12:13], ln_sml[:, 16:17], OP.subtract, OP.mult))
            r.drain("dve")
            r.wait("sync", lntok)
            r.dma_op("sync", lambda e, tbg=tbg, rt=rt: e.dma_start(
                out=scr_tm.ap()[128 * tbg:128 * (tbg + 1), :], in_=rt.all()))
            r.wait("pe", lntok)
            r.wait("pe", pe_last.get(6))
            lnev = {}

            def ln_t(f):
                bk = 4 + f % 2
                r.wait("pe", pe_last.get(bk))
                r.op("pe", lambda e, f=f, bk=bk, rt=rt: e.transpose(
                    psum[bk][:, 0:128], rt[:, 128 * f:128 * (f + 1)], c_id128.all()))
                return r.tick_last("pe")

            def ln_gate(f):
                r.wait("pe", lnev[f])
                r.op("pe", lambda e, f=f: e.matmul(
                    psum[6][:, 0:8], lnb_hi[:, 128 * f:128 * (f + 1)],
                    gwt_hi[:, 8 * f:8 * (f + 1)], start=(f == 0), stop=False))
                r.op("pe", lambda e, f=f: e.matmul(
                    psum[6][:, 0:8], lnb_lo[:, 128 * f:128 * (f + 1)],
                    gwt_hi[:, 8 * f:8 * (f + 1)], start=False, stop=False))
                r.op("pe", lambda e, f=f: e.matmul(
                    psum[6][:, 0:8], lnb_hi[:, 128 * f:128 * (f + 1)],
                    gwt_lo[:, 8 * f:8 * (f + 1)], start=False, stop=False))

            ptks = {0: ln_t(0)}
            for f in range(8):
                bk = 4 + f % 2
                r.wait("act", ptks[f])
                htok = r.tick("act", lambda e, f=f, bk=bk: e.copy(
                    lnb_hi[:, 128 * f:128 * (f + 1)], psum[bk][:, 0:128]))
                r.wait("dve", htok)
                etok = r.tick("dve", lambda e, f=f, bk=bk: e.tensor_tensor(
                    lnb_lo[:, 128 * f:128 * (f + 1)], psum[bk][:, 0:128],
                    lnb_hi[:, 128 * f:128 * (f + 1)], OP.subtract))
                pe_last[bk] = etok
                lnev[f] = etok
                if f + 1 < 8:
                    ptks[f + 1] = ln_t(f + 1)
                if f > 0:
                    ln_gate(f - 1)
                r.wait("sync", etok)
                r.dma_op("sync", lambda e, f=f, tbg=tbg: e.dma_start(
                    out=scr16_hi.ap()[128 * f:128 * (f + 1), 128 * tbg:128 * (tbg + 1)],
                    in_=lnb_hi[:, 128 * f:128 * (f + 1)]))
                r.dma_op("sync", lambda e, f=f, tbg=tbg: e.dma_start(
                    out=scr16_lo.ap()[128 * f:128 * (f + 1), 128 * tbg:128 * (tbg + 1)],
                    in_=lnb_lo[:, 128 * f:128 * (f + 1)]))
            ln_gate(7)
            r.op("pe", lambda e: e.matmul(psum[6][:, 0:8], c_ones1[0:1, 0:128], c_lb.all(),
                                          start=False, stop=True))
            gptok = r.tick_last("pe")
            r.wait("act", gptok)
            sctok = r.tick("act", lambda e: e.activation(g_sml[:, 0:8], psum[6][:, 0:8], AF.Sigmoid))
            pe_last[6] = sctok
            gating_tb(tbg, sctok)
            r.wait("sync", r.dve_tok())
            sb = r.bump_after_dma("sync")
            r.wait("dve", sb)

        def gating_tb(tbg, sctok):
            sc_ = g_sml[:, 0:8]
            s1, s2 = g_sml[:, 8:9], g_sml[:, 9:10]
            w1, w2, ws = g_sml[:, 10:11], g_sml[:, 11:12], g_sml[:, 12:13]
            m1, m2 = g_sml[:, 16:24], g_sml[:, 24:32]
            pf, t8, sel = g_sml[:, 32:40], g_sml[:, 40:48], g_sml[:, 48:56]
            sumc = g_sml[:, 56:64]

            def d(fn):
                r.op("dve", fn)
                r.drain("dve")

            r.wait("dve", sctok)
            d(lambda e: e.reduce_max(s1, sc_, axis=AX))
            d(lambda e: e.tensor_scalar(m1, sc_, s1, None, OP.is_ge))
            d(lambda e: e.tensor_tensor_scan(pf, m1, m1, 0.0, OP.add, OP.bypass))
            d(lambda e: e.tensor_single_scalar(t8, pf, 1.5, OP.is_le))
            d(lambda e: e.tensor_tensor(m1, m1, t8, OP.mult))
            d(lambda e: e.scalar_tensor_tensor(t8, m1, -2.0, sc_, OP.mult, OP.add))
            d(lambda e: e.reduce_max(s2, t8, axis=AX))
            d(lambda e: e.tensor_scalar(m2, t8, s2, None, OP.is_ge))
            d(lambda e: e.tensor_tensor_scan(pf, m2, m2, 0.0, OP.add, OP.bypass))
            d(lambda e: e.tensor_single_scalar(t8, pf, 1.5, OP.is_le))
            d(lambda e: e.tensor_tensor(m2, m2, t8, OP.mult))
            d(lambda e: e.tensor_tensor(sel, m1, m2, OP.add))
            d(lambda e: e.tensor_tensor(ws, s1, s2, OP.add))
            d(lambda e: e.tensor_scalar_add(ws, ws, 1e-8))
            d(lambda e: e.reciprocal(ws, ws))
            d(lambda e: e.tensor_tensor(w1, s1, ws, OP.mult))
            d(lambda e: e.tensor_tensor(w2, s2, ws, OP.mult))
            d(lambda e: e.tensor_scalar(sumc, m1, w1, None, OP.mult))
            d(lambda e: e.tensor_scalar(t8, m2, w2, None, OP.mult))
            d(lambda e: e.tensor_tensor(sumc, sumc, t8, OP.add))
            d(lambda e: e.tensor_tensor(snacc[:, 0:8], snacc[:, 0:8], sumc, OP.add))
            d(lambda e: e.tensor_tensor(snacc[:, 8:16], snacc[:, 8:16], sel, OP.add))
            r.tick("dve", lambda e, tbg=tbg: e.tensor_copy(sel_all[:, 8 * tbg:8 * (tbg + 1)], sel))

        marks = []

        def mark():
            marks.append({e: len(r.ops[e]) for e in ENGS})

        mark()
        attention_pass("A")
        mark()

        toks = [r.dma_op("pool", lambda e: e.dma_start(
            out=ag_in.ap()[0:65536].rearrange("(p c) -> p c", p=128), in_=kv_tmp.all()))]
        toks.append(r.dma_op("pool", lambda e: e.dma_start(
            out=ag_in.ap()[65536:69632].rearrange("(p c) -> p c", p=128),
            in_=arena[0:128, car_k.off:car_k.off + 32])))
        toks.append(r.dma_op("pool", lambda e: e.dma_start(
            out=ag_in.ap()[69632:69648].rearrange("(p c) -> p c", p=16), in_=car_cs.all())))
        r.bump_after_dma("pool")
        cc1tok = r.cc_op(lambda e: e.collective_compute(
            "AllGather", OP.bypass, replica_groups=[list(range(n_cc))],
            ins=[ag_in.ap()[65536:69648].opt()],
            outs=[ag_out.ap()[65536 * n_cc:65536 * n_cc + 4112 * n_cc].opt()]))
        cc2tok = r.cc_op(lambda e: e.collective_compute(
            "AllGather", OP.bypass, replica_groups=[list(range(n_cc))],
            ins=[ag_in.ap()[0:65536].opt()], outs=[ag_out.ap()[0:65536 * n_cc].opt()]))
        kv_state["cc2"] = cc2tok
        for src, dst in ((scr_q, q_t), (scr_k, k_t), (scr_v, v_t)):
            r.dma_op("sync", lambda e, src=src, dst=dst: e.dma_start(
                out=dst.all(), in_=src.ap()[:, 0:4096]))
        hoist["rl0"] = r.bump_after_dma("sync")
        r.wait("pool", cc1tok)
        r.wait("dve", cc1tok)
        for q in range(n_cc):
            r.dma_op("pool", lambda e, q=q: e.dma_start(
                out=slotc[:, 32 * q:32 * (q + 1)],
                in_=ag_out.ap()[65536 * n_cc + q * 4112:65536 * n_cc + q * 4112 + 4096]
                .rearrange("(p c) -> p c", p=128)))
            r.dma_op("pool", lambda e, q=q: e.dma_start(
                out=slotcs[0:16, q:q + 1],
                in_=ag_out.ap()[65536 * n_cc + q * 4112 + 4096:65536 * n_cc + q * 4112 + 4112]
                .rearrange("(p c) -> p c", p=16)))
        ftok = r.bump_after_dma("pool")
        r.wait("dve", ftok)
        r.op("dve", lambda e: e.memset(kv_car.all(), 0.0))
        r.op("dve", lambda e: e.memset(car_k.all(), 0.0))
        r.op("dve", lambda e: e.memset(car_q.all(), 0.0))
        r.op("dve", lambda e: e.memset(car_kso.all(), 0.0))
        r.op("dve", lambda e: e.memset(car_qsi.all(), 0.0))
        r.op("dve", lambda e: e.memset(car_cs.all(), 0.0))
        r.drain("dve")
        for q in range(n_cc):
            r.op("dve", lambda e, q=q: e.scalar_tensor_tensor(
                arena[0:128, car_k.off:car_k.off + 32],
                slotc[:, 32 * q:32 * (q + 1)], c_selcar[:, q:q + 1],
                arena[0:128, car_k.off:car_k.off + 32], OP.mult, OP.add))
            r.drain("dve")
            r.op("dve", lambda e, q=q: e.scalar_tensor_tensor(
                car_cs.all(), slotcs[0:16, q:q + 1], c_selcar[0:16, q:q + 1],
                car_cs.all(), OP.mult, OP.add))
            r.drain("dve")
        r.tick("dve", lambda e: e.nop())
        r.barrier()
        mark()

        attention_pass("B")
        mark()

        r.wait("pe", pe_last.get(0))
        r.op("pe", lambda e: e.matmul(psum[0][0:1, 0:16], c_onescol.all(), snacc.all(),
                                      start=True, stop=True))
        ptok = r.tick_last("pe")
        r.wait("dve", ptok)
        r.tick("dve", lambda e: e.tensor_copy(g_sml[0:1, 64:80], psum[0][0:1, 0:16]))
        pe_last[0] = r.dve_tok()
        r.drain("dve")
        dtok = r.dve_tok()
        r.wait("pool", dtok)
        r.dma_op("pool", lambda e: e.dma_start(out=ar_in.ap()[0:1, :], in_=g_sml[0:1, 64:80]))
        r.bump_after_dma("pool")
        cctok = r.cc_op(lambda e: e.collective_compute(
            "AllReduce", OP.add, replica_groups=[list(range(n_cc))],
            ins=[ar_in.ap().opt()], outs=[ar_out.ap().opt()]))
        mark()
        r.wait("sync", r.pe_tok())
        r.wait("sync", r.dve_tok())
        r.wait("sync", split_state["spill"])
        r.wait("sync", split_state["sub"])
        ffn_dve_fence = ("act", r.cnt["act"])
        ar_state = {"catok": None}

        def ar_tail():
            r.wait("pool", cctok)
            r.dma_op("pool", lambda e: e.dma_start(out=g_sml[0:1, 64:80], in_=ar_out.ap()[0:1, :]))
            r.await_dma("dve", "pool")
            r.op("dve", lambda e: e.tensor_single_scalar(g_sml[0:1, 80:88], g_sml[0:1, 72:80], 1.0, OP.max))
            r.drain("dve")
            r.op("dve", lambda e: e.reciprocal(g_sml[0:1, 88:96], g_sml[0:1, 80:88]))
            r.drain("dve")
            r.op("dve", lambda e: e.tensor_tensor(g_sml[0:1, 80:88], g_sml[0:1, 64:72],
                                                  g_sml[0:1, 88:96], OP.mult))
            r.drain("dve")
            wtok = r.tick("dve", lambda e: e.nop())
            r.wait("pe", wtok)
            r.wait("pe", pe_last.get(1))
            r.op("pe", lambda e: e.matmul(psum[1][:, 0:8], c_ones1[0:1, 0:128], g_sml[0:1, 80:88],
                                          start=True, stop=True))
            ptok = r.tick_last("pe")
            r.wait("dve", ptok)
            r.tick("dve", lambda e: e.tensor_copy(wm_bc.all(), psum[1][:, 0:8]))
            pe_last[1] = r.dve_tok()
            r.drain("dve")
            for tbg in range(NTB):
                r.op("dve", lambda e, tbg=tbg: e.tensor_tensor(
                    c_all[:, 8 * tbg:8 * (tbg + 1)], sel_all[:, 8 * tbg:8 * (tbg + 1)],
                    wm_bc.all(), OP.mult))
            r.drain("dve")
            ar_state["catok"] = r.tick("dve", lambda e: e.nop())

        ln_pref = {}

        def prefetch_ln(tt, war_tok):
            b = tt % 2
            if war_tok is not None:
                r.wait("sync", war_tok)
            for m in range(8):
                r.dma_op("sync", lambda e, tt=tt, b=b, m=m: e.dma_start(
                    out=f_lnh[b][:, 512 * m:512 * (m + 1)],
                    in_=scr16_hi.ap()[128 * m:128 * (m + 1), 512 * tt:512 * (tt + 1)]))
                r.dma_op("sync", lambda e, tt=tt, b=b, m=m: e.dma_start(
                    out=f_lnl[b][:, 512 * m:512 * (m + 1)],
                    in_=scr16_lo.ap()[128 * m:128 * (m + 1), 512 * tt:512 * (tt + 1)]))
            ln_pref[tt] = (b, r.bump_after_dma("sync"))

        prefetch_ln(0, None)
        WSETS = (((f_wgh, f_wgl), (f_wuh, f_wul)), ((f2_wgh, f2_wgl), (f2_wuh, f2_wul)))
        wfence = {}
        dw_last_use = [None]

        def prefetch_gu(e_i):
            st_ = WSETS[e_i % 2]
            for mi, (dsth, dstl) in enumerate(st_):
                idx = 6 * e_i + 2 * mi
                r.dma_op("sync", lambda e, idx=idx, d=dsth: e.dma_start(
                    out=d.all(), in_=ffn16.ap()[idx, :, :]))
                r.dma_op("sync", lambda e, idx=idx, d=dstl: e.dma_start(
                    out=d.all(), in_=ffn16.ap()[idx + 1, :, :]))
            wfence[e_i] = r.bump_after_dma("sync")

        prefetch_gu(0)
        for e_i in range(NEXP):
            (wgh, wgl), (wuh, wul) = WSETS[e_i % 2]
            r.wait("sync", dw_last_use[0])
            idx = 6 * e_i + 4
            r.dma_op("sync", lambda e, idx=idx: e.dma_start(
                out=f_wdh.all(), in_=ffn16.ap()[idx, :, :]))
            r.dma_op("sync", lambda e, idx=idx: e.dma_start(
                out=f_wdl.all(), in_=ffn16.ap()[idx + 1, :, :]))
            dwf = r.bump_after_dma("sync")
            r.wait("pe", wfence[e_i])
            if e_i + 1 < NEXP:
                prefetch_gu(e_i + 1)
            for tt in range(NLT):
                b, ftok = ln_pref[tt]
                r.wait("pe", ftok)
                lnh, lnl = f_lnh[b], f_lnl[b]
                h_tok = {}

                def h_block(ib):
                    bg, bu = ib % 2, 2 + ib % 2
                    r.wait("pe", pe_last.get(bg))
                    r.wait("pe", pe_last.get(bu))
                    for wgt_h, wgt_l, bk in ((wgh, wgl, bg), (wuh, wul, bu)):
                        for m in range(8):
                            r.op("pe", lambda e, m=m, ib=ib, bk=bk, wh=wgt_h, lh=lnh: e.matmul(
                                psum[bk][:, 0:512],
                                wh[:, 512 * m + 128 * ib:512 * m + 128 * (ib + 1)],
                                lh[:, 512 * m:512 * (m + 1)], start=(m == 0), stop=False))
                            r.op("pe", lambda e, m=m, ib=ib, bk=bk, wh=wgt_h, ll=lnl: e.matmul(
                                psum[bk][:, 0:512],
                                wh[:, 512 * m + 128 * ib:512 * m + 128 * (ib + 1)],
                                ll[:, 512 * m:512 * (m + 1)], start=False, stop=False))
                            r.op("pe", lambda e, m=m, ib=ib, bk=bk, wl=wgt_l, lh=lnh: e.matmul(
                                psum[bk][:, 0:512],
                                wl[:, 512 * m + 128 * ib:512 * m + 128 * (ib + 1)],
                                lh[:, 512 * m:512 * (m + 1)], start=False,
                                stop=(m == 7)))
                    ptok = r.tick_last("pe")
                    r.wait("act", ptok)
                    stok = r.tick("act", lambda e, bg=bg: e.activation(
                        f_si.all(), psum[bg][:, 0:512], AF.Silu))
                    r.wait("dve", stok)
                    r.drain("dve")
                    r.op("dve", lambda e, bu=bu: e.tensor_tensor(
                        f_hf.all(), f_si.all(), psum[bu][:, 0:512], OP.mult))
                    r.drain("dve")
                    hftok = r.tick("dve", lambda e: e.nop())
                    r.wait("act", hftok)
                    hitok = r.tick("act", lambda e, ib=ib: e.copy(
                        f_hh[:, 512 * ib:512 * (ib + 1)], f_hf.all()))
                    r.wait("dve", hitok)
                    htok = r.tick("dve", lambda e, ib=ib: e.tensor_tensor(
                        f_hl[:, 512 * ib:512 * (ib + 1)], f_hf.all(),
                        f_hh[:, 512 * ib:512 * (ib + 1)], OP.subtract))
                    pe_last[bg] = stok
                    pe_last[bu] = htok
                    h_tok[ib] = htok

                def down_part(tbs, ihs):
                    for tb in tbs:
                        for nh in range(2):
                            bk = 4 + 2 * (tb % 2) + nh
                            if ihs[0] == 0:
                                r.wait("pe", pe_last.get(bk))
                            for ih in ihs:
                                r.op("pe", lambda e, ih=ih, tb=tb, nh=nh, bk=bk: e.matmul(
                                    psum[bk][:, 0:512],
                                    f_hh[:, 512 * ih + 128 * tb:512 * ih + 128 * (tb + 1)],
                                    f_wdh[:, 1024 * ih + 512 * nh:1024 * ih + 512 * (nh + 1)],
                                    start=(ih == 0), stop=False))
                                r.op("pe", lambda e, ih=ih, tb=tb, nh=nh, bk=bk: e.matmul(
                                    psum[bk][:, 0:512],
                                    f_hl[:, 512 * ih + 128 * tb:512 * ih + 128 * (tb + 1)],
                                    f_wdh[:, 1024 * ih + 512 * nh:1024 * ih + 512 * (nh + 1)],
                                    start=False, stop=False))
                                r.op("pe", lambda e, ih=ih, tb=tb, nh=nh, bk=bk: e.matmul(
                                    psum[bk][:, 0:512],
                                    f_hh[:, 512 * ih + 128 * tb:512 * ih + 128 * (tb + 1)],
                                    f_wdl[:, 1024 * ih + 512 * nh:1024 * ih + 512 * (nh + 1)],
                                    start=False, stop=(ih == 3)))

                def evac(tbs):
                    ptok = r.tick_last("pe")
                    if e_i == 0 and tt == 0:
                        r.wait("dve", ar_state["catok"])
                    for tb in tbs:
                        tbg = 4 * tt + tb
                        r.wait("dve", ptok)
                        for nh in range(2):
                            bk = 4 + 2 * (tb % 2) + nh
                            ccol = (c_all[:, 8 * tbg + e_i:8 * tbg + e_i + 1] if e_i < 8
                                    else c_one[:, 0:1])
                            if e_i == 0:
                                etok = r.tick("dve", lambda e, bk=bk, tbg=tbg, nh=nh, ccol=ccol:
                                              e.tensor_scalar(
                                                  acc[:, 1024 * tbg + 512 * nh:1024 * tbg + 512 * (nh + 1)],
                                                  psum[bk][:, 0:512], ccol, None, OP.mult))
                            else:
                                etok = r.tick("dve", lambda e, bk=bk, tbg=tbg, nh=nh, ccol=ccol:
                                              e.scalar_tensor_tensor(
                                                  acc[:, 1024 * tbg + 512 * nh:1024 * tbg + 512 * (nh + 1)],
                                                  psum[bk][:, 0:512], ccol,
                                                  acc[:, 1024 * tbg + 512 * nh:1024 * tbg + 512 * (nh + 1)],
                                                  OP.mult, OP.add))
                            pe_last[bk] = etok
                        r.drain("dve")
                    return ptok

                for ib in range(4):
                    h_block(ib)
                if e_i == 0 and tt == 0:
                    ar_tail()
                    r.wait("dve", ffn_dve_fence)
                if tt == 0:
                    r.wait("pe", dwf)
                for ib in range(4):
                    r.wait("pe", h_tok[ib])
                nxt = tt + 1 if tt + 1 < NLT else (0 if e_i + 1 < NEXP else None)
                if nxt is not None:
                    prefetch_ln(nxt, r.pe_tok())
                for tb in range(4):
                    down_part((tb,), (0, 1))
                    down_part((tb,), (2, 3))
                    tile_done = evac((tb,))
                r.wait("sync", tile_done)
                if tt == NLT - 1:
                    dw_last_use[0] = tile_done

        rt2 = (View(arena, WR, 1024), View(arena, WR + 1024, 1024))
        xs2 = (View(arena, WR + 2048, 1024), View(arena, WR + 3072, 1024))
        lnm = (View(arena, ln_sml.off, 20), View(arena, g_sml.off, 20))
        ostate = {}

        def ln2_a(tbg):
            b = tbg % 2
            rt, xs, lm = rt2[b], xs2[b], lnm[b]
            r.wait("sync", ostate.get(("out", tbg - 2)))
            r.dma_op("sync", lambda e, tbg=tbg, xs=xs: e.dma_start(
                out=xs.all(), in_=scr_tm.ap()[128 * tbg:128 * (tbg + 1), :]))
            ld = r.bump_after_dma("sync")
            r.wait("dve", ld)
            r.wait("dve", ostate.get(("out", tbg - 2)))
            r.op("dve", lambda e, tbg=tbg, rt=rt, xs=xs: e.tensor_tensor(
                rt.all(), acc[:, 1024 * tbg:1024 * (tbg + 1)], xs.all(), OP.add))
            r.drain("dve")
            r.op("dve", lambda e, rt=rt, lm=lm: e.bn_stats(lm[:, 0:6], rt[:, 0:512]))
            r.op("dve", lambda e, rt=rt, lm=lm: e.bn_stats(lm[:, 6:12], rt[:, 512:1024]))
            r.drain("dve")
            r.op("dve", lambda e, lm=lm: e.bn_aggr(lm[:, 12:14], lm[:, 0:12]))
            r.drain("dve")
            r.op("dve", lambda e, lm=lm: e.tensor_scalar_add(lm[:, 14:15], lm[:, 13:14], 1e-5))
            r.drain("dve")
            ostate[("s", tbg)] = r.tick("dve", lambda e: e.nop())

        def ln2_b(tbg):
            b = tbg % 2
            rt, lm = rt2[b], lnm[b]
            r.wait("act", ostate[("s", tbg)])
            atok = r.tick("act", lambda e, lm=lm: e.activation(
                lm[:, 15:16], lm[:, 14:15], AF.Sqrt))
            r.wait("dve", atok)
            r.op("dve", lambda e, lm=lm: e.reciprocal(lm[:, 16:17], lm[:, 15:16]))
            r.drain("dve")
            otok = r.tick("dve", lambda e, rt=rt, lm=lm: e.tensor_scalar(
                rt.all(), rt.all(), lm[:, 12:13], lm[:, 16:17], OP.subtract, OP.mult))
            r.drain("dve")
            r.wait("sync", otok)
            r.dma_op("sync", lambda e, tbg=tbg, rt=rt: e.dma_start(
                out=dt["out"].ap()[128 * tbg:128 * (tbg + 1), :], in_=rt.all()))
            ostate[("out", tbg)] = r.bump_after_dma("sync")

        ln2_a(0)
        for tbg in range(NTB):
            if tbg + 1 < NTB:
                ln2_a(tbg + 1)
            ln2_b(tbg)
        r.wait("sync", ("sync_dma", r.dma["sync"]))
        r.wait("pool", ("pool_dma", r.dma["pool"]))

        if STOP > 0:
            m = marks[STOP - 1]
            for e in ENGS:
                r.ops[e] = r.ops[e][:m[e]]
            for e in ENGS:
                cnt = 0
                for kind, fn, semn, val in r.ops[e]:
                    if kind == "dma":
                        cnt = val
                r.dma[e] = cnt
            for tbg in range(NTB):
                r.dma_op("sync", lambda e, tbg=tbg: e.dma_start(
                    out=dt["out"].ap()[128 * tbg:128 * (tbg + 1), :], in_=x_tm_sb.all()))
            r.wait("sync", ("sync_dma", r.dma["sync"]))
            r.wait("pool", ("pool_dma", r.dma["pool"]))

        with nc.Block() as blk:
            def emit(eng_name):
                def body(e):
                    for kind, fn, semn, val in r.ops[eng_name]:
                        if kind == "op":
                            fn(e)
                        elif kind == "inc":
                            fn(e).then_inc(sems[semn], 1)
                        elif kind == "dma":
                            fn(e).then_inc(sems[semn], 16)
                        elif kind == "bump":
                            e.sem_inc(sems[semn], 1)
                        elif kind == "wait":
                            e.wait_ge(sems[semn], val)
                        elif kind == "drain":
                            e.drain()
                return body

            blk.tensor(emit("pe"))
            blk.scalar(emit("act"))
            blk.vector(emit("dve"))
            blk.gpsimd(emit("pool"))
            blk.sync(emit("sync"))

    return nc



def _constants(core, n_cc=NCORES):
    b, g = divmod(core, 2)
    offp = g * LH
    pos = np.arange(1, LH + 1, dtype=np.float64)
    norm = np.zeros((48, LH), dtype=np.float32)
    norm[0:16] = (pos + offp).astype(np.float32)
    norm[32:48] = (1.0 / (pos + offp)).astype(np.float32)
    jl = (np.arange(128)[:, None] <= np.arange(128)[None, :]).astype(np.float32)
    selcar = np.zeros((128, n_cc), dtype=np.float32)
    if g == 1 and n_cc == NCORES:
        selcar[:, core - 1] = 1.0
    obd16 = np.zeros((128, 128), dtype=np.float32)
    selp16 = np.zeros((16, 1024), dtype=np.float32)
    for p in range(8):
        obd16[0:64, 16 * p + 2 * p] = 1.0
        obd16[64:128, 16 * p + 2 * p + 1] = 1.0
        selp16[2 * p, 128 * p:128 * p + 64] = 1.0
        selp16[2 * p + 1, 128 * p + 64:128 * (p + 1)] = 1.0
    return {
        "c_id128": np.eye(128, dtype=np.float32),
        "c_mask": jl.copy(),
        "c_obd16": obd16,
        "c_selp16": selp16,
        "c_ones1": np.ones((1, 128), dtype=np.float32),
        "c_onescol": np.ones((128, 1), dtype=np.float32),
        "c_norm": norm,
        "c_selcar": selcar,
        "c_one": np.ones((128, 1), dtype=np.float32),
    }


def _stage(core, inp, n_cc=NCORES):
    b, g = divmod(core, 2)
    sl = slice(g * LH, (g + 1) * LH)
    x = np.asarray(inp["L0_attn_x"], np.float32)[b, sl]
    d = {
        "x_fm": np.ascontiguousarray(x.T),
        "x_tm": np.ascontiguousarray(x),
        "w_q": np.ascontiguousarray(np.asarray(inp["L0_attn_Wq"], np.float32)),
        "w_k": np.ascontiguousarray(np.asarray(inp["L0_attn_Wk"], np.float32)),
        "w_v": np.ascontiguousarray(np.asarray(inp["L0_attn_Wv"], np.float32)),
        "w_o": np.ascontiguousarray(np.asarray(inp["L0_attn_Wo"], np.float32)),
        "b_q": np.ascontiguousarray(np.asarray(inp["L0_attn_bq"], np.float32).reshape(8, 128).T),
        "b_k": np.ascontiguousarray(np.asarray(inp["L0_attn_bk"], np.float32).reshape(8, 128).T),
        "b_v": np.asarray(inp["L0_attn_bv"], np.float32).reshape(1, D),
        "b_o": np.asarray(inp["L0_attn_bo"], np.float32).reshape(1, D),
        "gw_t": np.ascontiguousarray(np.asarray(inp["L0_ffn_gate_w"], np.float32).T),
        "lb": np.asarray(inp["L0_ffn_lb_bias"], np.float32).reshape(1, 8),
    }
    eg = np.asarray(inp["L0_ffn_expert_gate_w"], np.float32)
    eu = np.asarray(inp["L0_ffn_expert_up_w"], np.float32)
    ed = np.asarray(inp["L0_ffn_expert_down_w"], np.float32)
    sg = np.asarray(inp["L0_ffn_shared_gate_w"], np.float32)
    su = np.asarray(inp["L0_ffn_shared_up_w"], np.float32)
    sd = np.asarray(inp["L0_ffn_shared_down_w"], np.float32)
    gw = np.empty((NEXP, D, IEXP), dtype=np.float32)
    uw = np.empty((NEXP, D, IEXP), dtype=np.float32)
    dw = np.empty((NEXP, IEXP, D), dtype=np.float32)
    for e in range(8):
        gw[e] = eg[e].T; uw[e] = eu[e].T; dw[e] = ed[e].T
    gw[8] = sg[0:512].T; gw[9] = sg[512:1024].T
    uw[8] = su[0:512].T; uw[9] = su[512:1024].T
    dw[8] = sd[:, 0:512].T; dw[9] = sd[:, 512:1024].T
    d["ffn_gw"] = gw; d["ffn_uw"] = uw; d["ffn_dw"] = dw
    d.update(_constants(core, n_cc))
    return d


def kernel(**inputs):
    nc = build_nc()
    in_maps = [_stage(c, inputs) for c in range(NCORES)]
    res = run_bass_kernel_spmd(nc, in_maps, core_ids=list(range(NCORES)))
    out = np.empty((B, L, D), dtype=np.float32)
    for c in range(NCORES):
        b, g = divmod(c, 2)
        out[b, g * LH:(g + 1) * LH] = res.results[c]["out"]
    return out


# version 27
# speedup vs baseline: 1.0051x; 1.0051x over previous
import numpy as np
import contextlib
import concourse.bass as bass
import concourse.mybir as mybir
from concourse.alu_op_type import AluOpType as OP
from concourse.bass_utils import run_bass_kernel_spmd

F32 = mybir.dt.float32
AF = mybir.ActivationFunctionType
AX = mybir.AxisListType.X

B, L, D = 4, 4096, 1024
NCORES = 8
LH = L // 2
NPAIR = 8
LT, NLT = 512, 4
NCHT = 4
NTB = 16
EPS = 1e-6
NEXP = 10
IEXP = 512
BLOB = 65536 + 4 * 1024 + 16

ENGS = ("pe", "act", "dve", "pool", "sync")
STOP = 0
SKIP = set()


class Rec:
    def __init__(self):
        self.ops = {e: [] for e in ENGS}
        self.cnt = {e: 0 for e in ENGS}
        self.dma = {e: 0 for e in ENGS}
        self.cc = 0

    def op(self, eng, fn):
        self.ops[eng].append(("op", fn, None, None))

    def tick(self, eng, fn):
        self.cnt[eng] += 1
        self.ops[eng].append(("inc", fn, eng, self.cnt[eng]))
        return (eng, self.cnt[eng])

    def tick_last(self, eng):
        kind, fn, semn, val = self.ops[eng][-1]
        if kind == "inc":
            return (semn, val)
        assert kind == "op", kind
        self.cnt[eng] += 1
        self.ops[eng][-1] = ("inc", fn, eng, self.cnt[eng])
        return (eng, self.cnt[eng])

    def dma_op(self, eng, fn):
        self.dma[eng] += 16
        self.ops[eng].append(("dma", fn, eng + "_dma", self.dma[eng]))
        return (eng + "_dma", self.dma[eng])

    def cc_op(self, fn):
        self.cc += 1
        self.ops["pool"].append(("inc", fn, "cc", self.cc))
        return ("cc", self.cc)

    def bump_after_dma(self, eng):
        self.ops[eng].append(("wait", None, eng + "_dma", self.dma[eng]))
        self.cnt[eng] += 1
        self.ops[eng].append(("bump", None, eng, self.cnt[eng]))
        return (eng, self.cnt[eng])

    def await_dma(self, consumer, eng):
        tok = self.bump_after_dma(eng)
        self.wait(consumer, tok)
        return tok

    def fence_dma_at(self, eng, dtok):
        self.ops[eng].append(("wait", None, dtok[0], dtok[1]))
        self.cnt[eng] += 1
        self.ops[eng].append(("bump", None, eng, self.cnt[eng]))
        return (eng, self.cnt[eng])

    def wait(self, eng, token):
        if token is not None:
            self.ops[eng].append(("wait", None, token[0], token[1]))

    def drain(self, eng):
        self.ops[eng].append(("drain", None, None, None))

    def pe_tok(self):
        return ("pe", self.cnt["pe"])

    def dve_tok(self):
        return ("dve", self.cnt["dve"])

    def barrier(self):
        toks = []
        for e in ENGS:
            if self.dma[e]:
                self.wait(e, (e + "_dma", self.dma[e]))
            if e in ("pe", "act", "dve", "pool"):
                self.ops[e].append(("drain", None, None, None))
            self.cnt[e] += 1
            self.ops[e].append(("bump", None, e, self.cnt[e]))
            toks.append((e, self.cnt[e]))
        for e in ENGS:
            for t in toks:
                if t[0] != e:
                    self.wait(e, t)


class View:
    def __init__(self, t, off, cols, rows=128, row0=0):
        self.t, self.off, self.cols, self.rows, self.row0 = t, off, cols, rows, row0

    def __getitem__(self, idx):
        r, c = idx
        if isinstance(c, slice):
            c0 = 0 if c.start is None else c.start
            c1 = self.cols if c.stop is None else c.stop
        else:
            c0, c1 = c, c + 1
        if isinstance(r, slice):
            r = slice((r.start or 0) + self.row0,
                      (self.rows if r.stop is None else r.stop) + self.row0)
        else:
            r = slice(r + self.row0, r + self.row0 + 1)
        return self.t[r, self.off + c0:self.off + c1]

    def all(self):
        return self.t[self.row0:self.row0 + self.rows, self.off:self.off + self.cols]


def build_nc(n_cc=NCORES):
    nc = bass.Bass("TRN2", target_bir_lowering=False)
    dt = {}

    def din(name, shape):
        dt[name] = nc.dram_tensor(name, shape, F32, kind="ExternalInput")

    def dout(name, shape):
        dt[name] = nc.dram_tensor(name, shape, F32, kind="ExternalOutput")

    din("x_fm", [D, LH]); din("x_tm", [LH, D])
    din("w_q", [D, D]); din("w_k", [D, D]); din("w_v", [D, D]); din("w_o", [D, D])
    din("b_q", [128, 8]); din("b_k", [128, 8])
    din("b_v", [1, D]); din("b_o", [1, D])
    din("gw_t", [D, 8]); din("lb", [1, 8])
    din("ffn_gw", [NEXP, D, IEXP]); din("ffn_uw", [NEXP, D, IEXP]); din("ffn_dw", [NEXP, IEXP, D])
    din("c_id128", [128, 128])
    din("c_mask", [128, 128])
    din("c_obd16", [128, 128])
    din("c_selp16", [16, 1024])
    din("c_ones1", [1, 128]); din("c_onescol", [128, 1])
    din("c_norm", [48, LH])
    din("c_selcar", [128, n_cc]); din("c_one", [128, 1])
    dout("out", [LH, D])

    BF16 = mybir.dt.bfloat16
    scr_tm = nc.dram_tensor("scr_tm", [LH, D], F32)
    scr16_hi = nc.dram_tensor("scr16_hi", [D, LH], BF16)
    scr16_lo = nc.dram_tensor("scr16_lo", [D, LH], BF16)
    ffn16 = nc.dram_tensor("ffn16", [NEXP * 6, 128, 4096], BF16)
    scr_q = nc.dram_tensor("scr_q", [128, NLT * 4096], F32)
    scr_k = nc.dram_tensor("scr_k", [128, NLT * 4096], F32)
    scr_v = nc.dram_tensor("scr_v", [128, NLT * 4096], F32)
    scr_dot = nc.dram_tensor("scr_dot", [32, LH], F32)
    ag_in = nc.dram_tensor("ag_in", [BLOB], F32)
    ag_out = nc.dram_tensor("ag_out", [n_cc * BLOB], F32)
    ar_in = nc.dram_tensor("ar_in", [1, 16], F32)
    ar_out = nc.dram_tensor("ar_out", [1, 16], F32)

    es = contextlib.ExitStack()
    with es:
        AR_W = 53200
        arena = es.enter_context(nc.sbuf_tensor("arena", [128, AR_W], F32))
        off = [0]

        def alloc(cols, rows=128, row0=0):
            o = off[0]
            off[0] += cols
            assert off[0] <= AR_W, f"arena overflow {off[0]}"
            return View(arena, o, cols, rows, row0)

        psum = [es.enter_context(nc.psum_tensor(f"ps{i}", [128, 512], F32)) for i in range(8)]
        sems = {}
        for e in ENGS:
            sems[e] = es.enter_context(nc.semaphore("s_" + e))
            sems[e + "_dma"] = es.enter_context(nc.semaphore("sd_" + e))
        sems["cc"] = es.enter_context(nc.semaphore("s_cc"))

        c_id128 = alloc(128)
        c_mask = alloc(128)
        c_obd16 = alloc(128)
        c_selp16 = alloc(1024, rows=16)
        c_ones1 = alloc(128, rows=1)
        c_onescol = alloc(1)
        c_norm = alloc(LH, rows=48)
        c_bq = alloc(8); c_bk = alloc(8)
        c_lb = alloc(8, rows=1)
        c_selcar = alloc(n_cc); c_one = alloc(1)
        c_gwt = alloc(64)

        w_q = alloc(8192); w_k = alloc(8192); w_v = alloc(8192)
        arena16 = arena.bitcast(mybir.dt.bfloat16)

        def V16(word_off, cols16, rows=128, row0=0):
            return View(arena16, 2 * word_off, cols16, rows, row0)

        WR = w_q.off
        r_tm4 = View(arena, WR, 4096)
        lnb_hi = V16(WR + 4096, 1024)
        lnb_lo = V16(WR + 4608, 1024)
        stg = View(arena, WR + 5120, 4096)
        st_hi = V16(WR + 9216, 4096)
        st_lo = V16(WR + 11264, 4096)
        wo16_hi = V16(WR + 13312, 8192)
        wo16_lo = V16(WR + 17408, 8192)
        r_tm = View(arena, WR + 512, 1024)
        x_tm_sb = View(arena, WR + 1536, 1024)
        f_lnh = [V16(WR + 0, 4096), V16(WR + 4096, 4096)]
        f_lnl = [V16(WR + 2048, 4096), V16(WR + 6144, 4096)]
        f_wgh = V16(WR + 8192, 4096); f_wgl = V16(WR + 10240, 4096)
        f_wuh = V16(WR + 12288, 4096); f_wul = V16(WR + 14336, 4096)
        f_wdh = V16(WR + 16384, 4096); f_wdl = V16(WR + 18432, 4096)
        f_hh = V16(WR + 20480, 2048); f_hl = V16(WR + 21504, 2048)

        x_t = alloc(4096); q_t = alloc(4096); k_t = alloc(4096); v_t = alloc(4096)
        acc = View(arena, x_t.off, 16384)
        gwt16 = alloc(64)
        gwt_hi = View(arena16, 2 * gwt16.off, 64)
        gwt_lo = View(arena16, 2 * gwt16.off + 64, 64)

        sc0 = alloc(512); sc1 = alloc(512); pr0 = alloc(512); pr1 = alloc(512)
        ktm = alloc(256); sct = alloc(64)
        kv_glob = alloc(512); kv_car = alloc(512); kv_tmp = alloc(512)
        chain = alloc(4096, rows=16)
        f_si = View(arena, kv_glob.off, 512)
        f_hf = View(arena, kv_car.off, 512)
        f2_wgh = V16(chain.off, 4096); f2_wgl = V16(chain.off + 2048, 4096)
        f2_wuh = V16(sc0.off, 4096); f2_wul = V16(WR + 22528, 4096)
        slot8 = View(arena, WR + 4608, 4096)
        slotc = View(arena, WR + 8704, 256)
        slotcs = View(arena, WR + 8960, 8, rows=16)
        q16_hi = V16(q_t.off, 4096)
        q16_lo = V16(q_t.off + 2048, 4096)
        wq16 = V16(w_q.off, 16384)
        wk16 = V16(w_k.off, 16384)
        wv16w = V16(w_v.off, 16384)
        xt16 = V16(x_t.off, 8192)
        c_bv = View(arena, chain.off, 1024, rows=1, row0=32)
        c_bo = View(arena, chain.off + 1024, 1024, rows=1, row0=32)
        c_ones32 = View(arena, chain.off + 2048, 128, rows=1, row0=32)
        st_col = alloc(48); st_cs = alloc(2, rows=16)
        car_k = alloc(8); car_q = alloc(8); car_kso = alloc(8); car_qsi = alloc(8)
        car_cs = alloc(1, rows=16); blobtmp = alloc(8)
        ln_sml = alloc(20); g_sml = alloc(96)
        sel_all = alloc(128); c_all = alloc(128); snacc = alloc(16); wm_bc = alloc(8)

        r = Rec()
        pe_last = {}

        def slot(i, c0=0, c1=512):
            return chain[0:16, 512 * i + c0:512 * i + c1]

        def nrows(kind, c0, c1):
            base = 0 if kind == "n" else 32
            return c_norm[base:base + 16, c0:c1] if kind == "n" else arena[32:48, c_norm.off + c0:c_norm.off + c1]

        def load(dst, src):
            return r.dma_op("sync", lambda e, d=dst, s=src: e.dma_start(out=d, in_=s))

        load(c_id128.all(), dt["c_id128"].ap())
        load(c_mask.all(), dt["c_mask"].ap())
        load(c_obd16.all(), dt["c_obd16"].ap())
        load(c_selp16.all(), dt["c_selp16"].ap())
        load(c_ones1.all(), dt["c_ones1"].ap())
        load(c_onescol.all(), dt["c_onescol"].ap())
        load(c_norm.all(), dt["c_norm"].ap())
        load(c_bq.all(), dt["b_q"].ap())
        load(c_bk.all(), dt["b_k"].ap())
        load(c_bv.all(), dt["b_v"].ap())
        load(c_bo.all(), dt["b_o"].ap())
        load(c_ones32.all(), dt["c_ones1"].ap())
        load(c_lb.all(), dt["lb"].ap())
        load(c_selcar.all(), dt["c_selcar"].ap())
        load(c_one.all(), dt["c_one"].ap())
        wtoks = {}
        for wn, wv in (("w_q", w_q), ("w_k", w_k), ("w_v", w_v)):
            for m in range(8):
                load(wv[:, 1024 * m:1024 * (m + 1)], dt[wn].ap()[128 * m:128 * (m + 1), :])
            wtoks[wv.off] = r.bump_after_dma("sync")
        for m in range(8):
            load(c_gwt[:, 8 * m:8 * m + 8], dt["gw_t"].ap()[128 * m:128 * (m + 1), :])
        ftok = r.bump_after_dma("sync")

        r.op("dve", lambda e: e.memset(snacc.all(), 0.0))
        r.op("dve", lambda e: e.memset(kv_car.all(), 0.0))
        r.op("dve", lambda e: e.memset(kv_glob.all(), 0.0))
        r.op("dve", lambda e: e.memset(st_col.all(), 0.0))
        r.op("dve", lambda e: e.memset(st_cs.all(), 0.0))
        r.op("dve", lambda e: e.memset(
            View(arena, c_norm.off, 512, rows=1, row0=64).all(), 0.0))
        r.drain("dve")
        wscr = View(arena, q_t.off, 1024)
        prev_sub = None
        for wv_ in (w_q, w_k, w_v):
            wv16b = V16(wv_.off, 16384)
            r.wait("act", wtoks[wv_.off])
            for m in range(8):
                r.wait("act", prev_sub)
                r.tick("act", lambda e, wv_=wv_, m=m: e.copy(
                    wscr.all(), wv_[:, 1024 * m:1024 * (m + 1)]))
                r.drain("act")
                htok = r.tick("act", lambda e, wv16b=wv16b, m=m: e.copy(
                    wv16b[:, 2048 * m:2048 * m + 1024], wscr.all()))
                r.wait("dve", htok)
                prev_sub = r.tick("dve", lambda e, wv16b=wv16b, m=m: e.tensor_tensor(
                    wv16b[:, 2048 * m + 1024:2048 * (m + 1)], wscr.all(),
                    wv16b[:, 2048 * m:2048 * m + 1024], OP.subtract))
        r.drain("dve")
        r.wait("act", ftok)
        gt1 = r.tick("act", lambda e: e.copy(gwt_hi.all(), c_gwt.all()))
        r.wait("dve", gt1)
        r.tick("dve", lambda e: e.tensor_tensor(gwt_lo.all(), c_gwt.all(), gwt_hi.all(), OP.subtract))
        r.drain("dve")
        r.barrier()

        def scan(dst, src, init_ap, init_imm):
            if init_ap is None:
                return r.tick("dve", lambda e, d=dst, s=src, v=init_imm:
                              e.tensor_tensor_scan(d, s, s, v, OP.add, OP.bypass))
            return r.tick("dve", lambda e, d=dst, s=src, ia=init_ap:
                          e.tensor_tensor_scan(d, s, s, ia, OP.add, OP.bypass))

        def dotmm(bank, rows0, p, prod_ap, start, stop):
            tp = None if rows0 == 0 else (0, rows0)
            r.op("pe", lambda e, bank=bank, rows0=rows0, p=p, pa=prod_ap, start=start, stop=stop, tp=tp:
                 e.matmul(psum[bank][rows0:rows0 + 16, 0:512],
                          c_obd16[:, 16 * p:16 * (p + 1)], pa,
                          start=start, stop=stop, tile_position=tp))

        z16l = V16(c_norm.off, 128, rows=1, row0=64)
        z16r = V16(c_norm.off, 1024, rows=1, row0=64)

        def grp(bank, open_, cols=512, rows=128):
            r.op("pe", lambda e, bank=bank, open_=open_, cols=cols, rows=rows: e.matmul(
                psum[bank][0:rows, 0:cols], z16l[0:1, 0:128], z16r[0:1, 0:cols],
                start=open_, stop=not open_))

        def bc16(bank, src16, p, cols=512, o=0):
            r.op("pe", lambda e, bank=bank, s=src16, p=p, cols=cols, o=o:
                 e.matmul(psum[bank][:, o:o + cols], c_selp16[0:16, 128 * p:128 * (p + 1)], s,
                          start=True, stop=True))

        split_state = {"sub": None, "spill": None}
        vs_state = {"tok": None}
        hoist = {}
        kv_state = {}
        wo_state = {"ready": None}

        def wo_split():
            for half in range(2):
                r.wait("pool", split_state["sub"])
                r.wait("pool", split_state["spill"])
                for m in range(4):
                    r.dma_op("pool", lambda e, half=half, m=m: e.dma_start(
                        out=stg[:, 1024 * m:1024 * (m + 1)],
                        in_=dt["w_o"].ap()[512 * half + 128 * m:512 * half + 128 * (m + 1), :]))
                r.bump_after_dma("pool")
                htok = r.tick("pool", lambda e, half=half: e.tensor_copy(
                    wo16_hi[:, 4096 * half:4096 * (half + 1)], stg.all()))
                r.drain("pool")
                gtok = r.tick("pool", lambda e, half=half: e.tensor_sub(
                    wo16_lo[:, 4096 * half:4096 * (half + 1)], stg.all(),
                    wo16_hi[:, 4096 * half:4096 * (half + 1)]))
                split_state["sub"] = gtok
                split_state["spill"] = htok
            wo_state["ready"] = split_state["sub"]

        def ffn_split_batch(mats):
            if "split" in SKIP:
                return
            for i in mats:
                e_i, kind = divmod(i, 3)
                src = (dt["ffn_gw"], dt["ffn_uw"], dt["ffn_dw"])[kind]
                r.wait("pool", split_state["sub"])
                r.wait("pool", split_state["spill"])
                if kind < 2:
                    for m in range(8):
                        r.dma_op("pool", lambda e, src=src, e_i=e_i, m=m: e.dma_start(
                            out=stg[:, 512 * m:512 * (m + 1)],
                            in_=src.ap()[e_i, 128 * m:128 * (m + 1), :]))
                else:
                    for i4 in range(4):
                        r.dma_op("pool", lambda e, src=src, e_i=e_i, i4=i4: e.dma_start(
                            out=stg[:, 1024 * i4:1024 * (i4 + 1)],
                            in_=src.ap()[e_i, 128 * i4:128 * (i4 + 1), :]))
                r.bump_after_dma("pool")
                htok = r.tick("pool", lambda e: e.tensor_copy(st_hi.all(), stg.all()))
                r.wait("pool", htok)
                gtok = r.tick("pool", lambda e: e.tensor_sub(st_lo.all(), stg.all(), st_hi.all()))
                split_state["sub"] = gtok
                r.wait("pool", gtok)
                idx = 2 * i
                r.dma_op("pool", lambda e, idx=idx: e.dma_start(
                    out=ffn16.ap()[idx, :, :], in_=st_hi.all()))
                r.dma_op("pool", lambda e, idx=idx: e.dma_start(
                    out=ffn16.ap()[idx + 1, :, :], in_=st_lo.all()))
                split_state["spill"] = r.bump_after_dma("pool")

        def attention_pass(phase):
            isB = phase == "B"

            if isB:
                r.op("dve", lambda e: e.tensor_scalar_add(st_col[:, 32:40], car_kso.all(), EPS))
                r.op("dve", lambda e: e.tensor_scalar_add(st_col[:, 40:48], car_qsi.all(), EPS))
                r.op("dve", lambda e: e.tensor_copy(st_cs[0:16, 1:2], car_cs.all()))
                r.drain("dve")
                r.tick("dve", lambda e: e.nop())
                r.barrier()
                r.wait("sync", kv_state["cc2"])
                for q in range(n_cc):
                    r.dma_op("sync", lambda e, q=q: e.dma_start(
                        out=r_tm4[:, 512 * q:512 * (q + 1)],
                        in_=ag_out.ap()[65536 * q:65536 * (q + 1)]
                        .rearrange("(p c) -> p c", p=128)))
                kv_state["dma"] = r.bump_after_dma("sync")

            xsp_state = {}

            def x_split(jj):
                for m in range(8):
                    r.dma_op("sync", lambda e, m=m, jj=jj: e.dma_start(
                        out=x_t[:, 512 * m:512 * (m + 1)],
                        in_=dt["x_fm"].ap()[128 * m:128 * (m + 1), LT * jj:LT * (jj + 1)]))
                xltok = r.bump_after_dma("sync")
                dvetok = r.dve_tok()
                r.wait("act", xltok)
                r.wait("act", dvetok)
                sprev = {}
                for m in range(8):
                    scb = sc0 if m % 2 == 0 else sc1
                    r.wait("act", sprev.get(m % 2))
                    r.tick("act", lambda e, m=m, scb=scb: e.copy(
                        scb.all(), x_t[:, 512 * m:512 * (m + 1)]))
                    r.drain("act")
                    htok = r.tick("act", lambda e, m=m, scb=scb: e.copy(
                        xt16[:, 1024 * m:1024 * m + 512], scb.all()))
                    r.wait("dve", htok)
                    sprev[m % 2] = r.tick("dve", lambda e, m=m, scb=scb: e.tensor_tensor(
                        xt16[:, 1024 * m + 512:1024 * (m + 1)], scb.all(),
                        xt16[:, 1024 * m:1024 * m + 512], OP.subtract))
                r.drain("dve")
                xsp_state[jj] = r.tick("dve", lambda e: e.nop())

            if not isB:
                x_split(0)
            for j in range(NLT):
                if isB:
                    if j == 0:
                        wo_split()
                    ffn_split_batch(range(8 * j, min(8 * j + 8, 3 * NEXP)))
                if not isB:
                    xsp = xsp_state.pop(j)
                    r.wait("pe", xsp)
                    qk_etok = {}
                    for f in range(8):
                        for wi, (w16, dst, bias) in enumerate(((wq16, q_t, c_bq), (wk16, k_t, c_bk))):
                            bk = 2 * wi + f % 2
                            r.wait("pe", pe_last.get(bk))
                            for pas in range(3):
                                wo_, xo_ = ((0, 0), (0, 512), (1024, 0))[pas]
                                for m in range(8):
                                    r.op("pe", lambda e, w16=w16, m=m, f=f, bk=bk,
                                         wo_=wo_, xo_=xo_, pas=pas: e.matmul(
                                        psum[bk][:, 0:512],
                                        w16[:, 2048 * m + wo_ + 128 * f:
                                            2048 * m + wo_ + 128 * (f + 1)],
                                        xt16[:, 1024 * m + xo_:1024 * m + xo_ + 512],
                                        start=(pas == 0 and m == 0),
                                        stop=(pas == 2 and m == 7)))
                            ptok = r.tick_last("pe")
                            r.wait("act", ptok)
                            etok = r.tick("act", lambda e, bk=bk, dst=dst, f=f, bias=bias:
                                          e.activation(dst[:, 512 * f:512 * (f + 1)],
                                                       psum[bk][:, 0:512], AF.Sigmoid,
                                                       bias=bias[:, f:f + 1], scale=1.0))
                            pe_last[bk] = etok
                            qk_etok[(wi, f)] = etok
                            r.wait("sync", etok)
                            r.dma_op("sync", lambda e, dst=dst, f=f, j=j, wi=wi: e.dma_start(
                                out=(scr_q if wi == 0 else scr_k).ap()[
                                    :, 4096 * j + 512 * f:4096 * j + 512 * (f + 1)],
                                in_=dst[:, 512 * f:512 * (f + 1)]))
                    def v_mm(tbs):
                        ptokv = {}
                        for tb in tbs:
                            for hf in range(2):
                                bk = (2 * tb + hf) % 4
                                r.wait("pe", pe_last.get(bk))
                                for pas in range(3):
                                    xo_, wo_ = ((0, 0), (512, 0), (0, 1024))[pas]
                                    for m in range(8):
                                        r.op("pe", lambda e, m=m, tb=tb, hf=hf, bk=bk,
                                             xo_=xo_, wo_=wo_, pas=pas: e.matmul(
                                            psum[bk][:, 0:512],
                                            xt16[:, 1024 * m + xo_ + 128 * tb:
                                                 1024 * m + xo_ + 128 * (tb + 1)],
                                            wv16w[:, 2048 * m + wo_ + 512 * hf:
                                                  2048 * m + wo_ + 512 * (hf + 1)],
                                            start=(pas == 0 and m == 0), stop=False))
                                r.op("pe", lambda e, hf=hf, bk=bk: e.matmul(
                                    psum[bk][:, 0:512], c_ones32.all(),
                                    c_bv[0:1, 512 * hf:512 * (hf + 1)], start=False, stop=True))
                                ptokv[(tb, hf)] = r.tick_last("pe")
                        return ptokv

                    def v_evac(tbs, ptokv):
                        for tb in tbs:
                            for hf in range(2):
                                bk = (2 * tb + hf) % 4
                                r.wait("dve", ptokv[(tb, hf)])
                                etok = r.tick("dve", lambda e, bk=bk, tb=tb, hf=hf: e.tensor_copy(
                                    v_t[:, 1024 * tb + 512 * hf:1024 * tb + 512 * (hf + 1)],
                                    psum[bk][:, 0:512]))
                                pe_last[bk] = etok
                                r.wait("sync", etok)
                                r.dma_op("sync", lambda e, tb=tb, hf=hf, j=j: e.dma_start(
                                    out=scr_v.ap()[:, 4096 * j + 1024 * tb + 512 * hf:
                                                   4096 * j + 1024 * tb + 512 * (hf + 1)],
                                    in_=v_t[:, 1024 * tb + 512 * hf:
                                            1024 * tb + 512 * (hf + 1)]))

                    pt1 = v_mm((0, 1))
                    v_evac((0, 1), pt1)
                    pt2 = v_mm((2, 3))
                    s1adot = {}
                    for p in range(NPAIR):
                        qv = q_t[:, 512 * p:512 * (p + 1)]
                        kvw = k_t[:, 512 * p:512 * (p + 1)]
                        r.wait("dve", qk_etok[(0, p)])
                        r.wait("dve", qk_etok[(1, p)])
                        for si, (src, other, dbank, scb, so_) in enumerate(
                                ((kvw, qv, 6, sc0, 0), (qv, kvw, 5, sc1, 8))):
                            prb = (pr0, kv_tmp, pr1, kv_car)[2 * si + p % 2]
                            r.wait("dve", s1adot.get((p - 2, si)))
                            scan(scb.all(), src,
                                 None if j == 0 else st_col[:, so_ + p:so_ + p + 1], EPS)
                            r.drain("dve")
                            r.op("dve", lambda e, scb=scb, so_=so_, p=p: e.tensor_copy(
                                st_col[:, so_ + p:so_ + p + 1], scb[:, 511:512]))
                            ptk = r.tick("dve", lambda e, prb=prb, o=other, scb=scb:
                                         e.scalar_tensor_tensor(prb.all(), o, EPS, scb.all(),
                                                                OP.add, OP.mult))
                            r.drain("dve")
                            r.wait("pe", ptk)
                            if p == 0:
                                r.wait("pe", pe_last.get(dbank))
                            dotmm(dbank, 0, p, prb.all(),
                                  start=(p == 0), stop=(p == NPAIR - 1))
                            s1adot[(p, si)] = r.tick_last("pe")
                    v_evac((2, 3), pt2)
                    sb = r.bump_after_dma("sync")
                    r.wait("dve", sb)
                else:
                    if j == 0 and hoist.get("rl0") is not None:
                        ftok = hoist["rl0"]
                    else:
                        for src, dst in ((scr_q, q_t), (scr_k, k_t), (scr_v, v_t)):
                            r.dma_op("sync", lambda e, src=src, dst=dst, j=j: e.dma_start(
                                out=dst.all(), in_=src.ap()[:, 4096 * j:4096 * (j + 1)]))
                        ftok = r.bump_after_dma("sync")
                    r.wait("dve", ftok)
                    r.wait("pe", ftok)

                s1dot = {}
                for p in range(NPAIR if isB else 0):
                    qv = q_t[:, 512 * p:512 * (p + 1)]
                    kvw = k_t[:, 512 * p:512 * (p + 1)]
                    if True:
                        for si, (src, car, dbank) in enumerate(((qv, car_k, 6), (kvw, car_q, 5))):
                            prb = (pr0, pr1, sc0, sc1)[2 * (p % 2) + si]
                            r.wait("dve", s1dot.get((p - 2, si)))
                            ptk = r.tick("dve", lambda e, src=src, car=car, p=p, prb=prb:
                                         e.tensor_scalar(prb.all(), src, EPS, car[:, p:p + 1],
                                                         OP.add, OP.mult))
                            r.drain("dve")
                            r.wait("pe", ptk)
                            if p == 0:
                                r.wait("pe", pe_last.get(dbank))
                            dotmm(dbank, 0, p, prb.all(),
                                  start=(p == 0), stop=(p == NPAIR - 1))
                            s1dot[(p, si)] = r.tick_last("pe")
                ptok = r.pe_tok()
                r.wait("dve", ptok)
                if not isB:
                    r.tick("dve", lambda e: e.tensor_copy(slot(0), psum[6][0:16, 0:512]))
                    r.tick("dve", lambda e: e.tensor_copy(slot(1), psum[5][0:16, 0:512]))
                    pe_last[6] = r.dve_tok()
                    pe_last[5] = r.dve_tok()
                    r.drain("dve")
                    dtk = r.dve_tok()
                    r.wait("sync", dtk)
                    r.dma_op("sync", lambda e, j=j: e.dma_start(
                        out=scr_dot.ap()[0:16, LT * j:LT * (j + 1)], in_=slot(0)))
                    r.dma_op("sync", lambda e, j=j: e.dma_start(
                        out=scr_dot.ap()[16:32, LT * j:LT * (j + 1)], in_=slot(1)))
                    den_q, den_k = slot(0), slot(1)
                else:
                    r.dma_op("sync", lambda e, j=j: e.dma_start(
                        out=slot(0), in_=scr_dot.ap()[0:16, LT * j:LT * (j + 1)]))
                    r.dma_op("sync", lambda e, j=j: e.dma_start(
                        out=slot(1), in_=scr_dot.ap()[16:32, LT * j:LT * (j + 1)]))
                    r.await_dma("dve", "sync")
                    r.tick("dve", lambda e: e.tensor_tensor(slot(5), slot(0), psum[6][0:16, 0:512], OP.add))
                    r.tick("dve", lambda e: e.tensor_tensor(slot(6), slot(1), psum[5][0:16, 0:512], OP.add))
                    r.drain("dve")
                    den_q, den_k = slot(5), slot(6)
                pe_last[6] = r.dve_tok()
                pe_last[5] = r.dve_tok()
                r.op("dve", lambda e, d=den_q: e.reciprocal(slot(2), d))
                r.op("dve", lambda e, d=den_k: e.reciprocal(slot(7), d))
                r.drain("dve")
                r.op("dve", lambda e, j=j: e.tensor_tensor(
                    slot(3), slot(2), nrows("n", LT * j, LT * (j + 1)), OP.mult))
                r.tick("dve", lambda e, j=j: e.tensor_tensor(
                    slot(4), slot(7), nrows("n", LT * j, LT * (j + 1)), OP.mult))
                r.drain("dve")

                s3tok = {}
                s3dot = {}
                s5tok = {}
                s5dot = {}
                spe0 = r.pe_tok()

                def s3_bc(p):
                    bA = 5 if p % 2 == 0 else 0
                    r.wait("pe", pe_last.get(bA))
                    bc16(bA, slot(4), p)
                    s3tok[p] = r.tick_last("pe")

                def s5_bc(p):
                    bA = 1 if p % 2 == 0 else 3
                    r.wait("pe", pe_last.get(bA))
                    bc16(bA, slot(3), p)
                    if isB:
                        bB = 2 if p % 2 == 0 else 7
                        r.wait("pe", pe_last.get(bB))
                        bc16(bB, slot(2), p)
                    s5tok[p] = r.tick_last("pe")

                r.wait("pe", r.dve_tok())
                s3_bc(0)
                s5_bc(0)
                for p in range(NPAIR):
                    b3 = 5 if p % 2 == 0 else 0
                    b5 = 1 if p % 2 == 0 else 3
                    bB = 2 if p % 2 == 0 else 7
                    kvw = k_t[:, 512 * p:512 * (p + 1)]
                    qv = q_t[:, 512 * p:512 * (p + 1)]
                    pr3 = pr0 if p % 2 == 0 else pr1
                    pr5 = kv_tmp if p % 2 == 0 else kv_car
                    r.wait("dve", s3tok[p])
                    r.wait("dve", s3dot.get(p - 2, spe0 if p < 2 else None))
                    sot = r.tick("dve", lambda e, kvw=kvw, b3=b3, pr3=pr3: e.tensor_tensor(
                        pr3.all(), kvw, psum[b3][:, 0:512], OP.mult))
                    pe_last[b3] = sot
                    if p + 1 < NPAIR:
                        s3_bc(p + 1)
                    r.drain("dve")
                    scan(sc0.all(), pr3.all(),
                         st_col[:, 32 + p:33 + p] if (isB and j == 0)
                         else (None if j == 0 else st_col[:, 16 + p:17 + p]), EPS)
                    r.drain("dve")
                    r.op("dve", lambda e, p=p: e.tensor_copy(st_col[:, 16 + p:17 + p], sc0[:, 511:512]))
                    if isB:
                        ptk3 = r.tick("dve", lambda e, qv=qv, pr3=pr3: e.scalar_tensor_tensor(
                            pr3.all(), qv, EPS, sc0.all(), OP.add, OP.mult))
                        r.drain("dve")
                        r.wait("pe", ptk3)
                        if p == 0:
                            r.wait("pe", pe_last.get(6))
                        dotmm(6, 0, p, pr3.all(), start=(p == 0), stop=(p == NPAIR - 1))
                        s3dot[p] = r.tick_last("pe")
                    r.wait("dve", s5tok[p])
                    r.wait("dve", s5dot.get(p - 2, spe0 if p < 2 else None))
                    qsit = r.tick("dve", lambda e, qv=qv, b5=b5, pr5=pr5: e.tensor_tensor(
                        pr5.all(), qv, psum[b5][:, 0:512], OP.mult))
                    pe_last[b5] = qsit
                    if isB:
                        qst = r.tick("dve", lambda e, qv=qv, p=p, bB=bB: e.tensor_tensor(
                            x_t[:, 512 * p:512 * (p + 1)], qv, psum[bB][:, 0:512], OP.mult))
                        pe_last[bB] = qst
                    if p + 1 < NPAIR:
                        s5_bc(p + 1)
                    r.drain("dve")
                    scan(sc1.all(), pr5.all(),
                         st_col[:, 40 + p:41 + p] if (isB and j == 0)
                         else (None if j == 0 else st_col[:, 24 + p:25 + p]), EPS)
                    r.drain("dve")
                    r.op("dve", lambda e, p=p: e.tensor_copy(st_col[:, 24 + p:25 + p], sc1[:, 511:512]))
                    ptk5 = r.tick("dve", lambda e, kvw=kvw, pr5=pr5: e.scalar_tensor_tensor(
                        pr5.all(), kvw, EPS, sc1.all(), OP.add, OP.mult))
                    r.drain("dve")
                    r.wait("pe", ptk5)
                    if p == 0:
                        r.wait("pe", pe_last.get(4))
                    dotmm(4, 0, p, pr5.all(), start=(p == 0), stop=(p == NPAIR - 1))
                    s5dot[p] = r.tick_last("pe")
                if isB:
                    ptok = r.pe_tok()
                    r.wait("dve", ptok)
                    r.tick("dve", lambda e, j=j: e.tensor_tensor(
                        slot(5), psum[6][0:16, 0:512], nrows("r", LT * j, LT * (j + 1)), OP.mult))
                    pe_last[6] = r.dve_tok()
                    r.drain("dve")
                    dtk = r.dve_tok()
                    r.wait("act", dtk)
                    atok = r.tick("act", lambda e: e.activation(slot(7), slot(5), AF.Sigmoid))
                    r.wait("dve", atok)
                ptok = r.pe_tok()
                r.wait("dve", ptok)
                r.tick("dve", lambda e, j=j: e.tensor_tensor(
                    slot(6), psum[4][0:16, 0:512], nrows("r", LT * j, LT * (j + 1)), OP.mult))
                pe_last[4] = r.dve_tok()
                r.drain("dve")
                dtk = r.dve_tok()
                r.wait("act", dtk)
                atok = r.tick("act", lambda e: e.activation(slot(5), slot(6), AF.Exp))
                r.wait("dve", atok)
                scan(slot(6), slot(5),
                     st_cs[0:16, 1:2] if (isB and j == 0)
                     else (None if j == 0 else st_cs[0:16, 0:1]), 0.0)
                r.drain("dve")
                r.op("dve", lambda e: e.tensor_copy(
                    st_cs[0:16, 0:1], chain[0:16, 512 * 6 + 511:512 * 6 + 512]))
                r.drain("dve")
                r.op("dve", lambda e: e.reciprocal(slot(2), slot(6)))
                r.drain("dve")
                r.op("dve", lambda e: e.tensor_tensor(slot(3), slot(2), slot(5), OP.mult))
                r.drain("dve")
                r.tick("dve", lambda e, j=j: e.tensor_tensor(
                    slot(5), slot(3), nrows("n", LT * j, LT * (j + 1)), OP.mult))
                r.drain("dve")
                def sct_t(c):
                    bk = 5 if c % 2 == 0 else 0
                    r.wait("pe", pe_last.get(bk))
                    r.op("pe", lambda e, c=c, bk=bk: e.transpose(
                        psum[bk][0:128, 0:16], slot(5, 128 * c, 128 * (c + 1)),
                        c_id128[0:16, 0:16]))
                    return r.tick_last("pe"), bk

                r.wait("pe", r.dve_tok())
                tts = {0: sct_t(0)}
                for c in range(NCHT):
                    ptk, bk = tts[c]
                    r.wait("dve", ptk)
                    etok = r.tick("dve", lambda e, c=c, bk=bk: e.tensor_copy(
                        sct[:, 16 * c:16 * (c + 1)], psum[bk][0:128, 0:16]))
                    pe_last[bk] = etok
                    if c + 1 < NCHT:
                        tts[c + 1] = sct_t(c + 1)
                    r.wait("act", etok)
                    for h in range(16):
                        tick = (h == 15)
                        fn = lambda e, c=c, h=h: e.mul(
                            v_t[:, 1024 * c + 64 * h:1024 * c + 64 * (h + 1)],
                            v_t[:, 1024 * c + 64 * h:1024 * c + 64 * (h + 1)],
                            sct[:, 16 * c + h:16 * c + h + 1])
                        if tick:
                            vstok = r.tick("act", fn)
                            vs_state[c] = vstok
                            vs_state["tok"] = vstok
                        else:
                            r.op("act", fn)

                if not isB:
                    if j == 0:
                        r.wait("pe", pe_last.get(7))
                        grp(7, True)
                    kvseq = [(c, p) for c in range(NCHT) for p in range(NPAIR)]
                    ev = {}

                    def kv_t(i):
                        c, p = kvseq[i]
                        bk = 5 if i % 2 == 0 else 0
                        r.wait("pe", ev.get(i - 2, pe_last.get(bk) if i < 2 else None))
                        if i < 2 or kvseq[i - 1][0] != c:
                            r.wait("pe", vs_state[c])
                        r.op("pe", lambda e, p=p, c=c, bk=bk: e.transpose(
                            psum[bk][:, 0:128],
                            k_t[:, 512 * p + 128 * c:512 * p + 128 * (c + 1)], c_id128.all()))
                        return r.tick_last("pe")

                    ttok = {0: kv_t(0)}
                    for i in range(len(kvseq)):
                        c, p = kvseq[i]
                        par = i % 2
                        bk = 5 if par == 0 else 0
                        r.wait("dve", ttok[i])
                        ev[i] = r.tick("dve", lambda e, par=par, bk=bk: e.tensor_copy(
                            ktm[:, 128 * par:128 * (par + 1)], psum[bk][:, 0:128]))
                        if i + 1 < len(kvseq):
                            ttok[i + 1] = kv_t(i + 1)
                        r.wait("pe", ev[i])
                        for hh in range(2):
                            h = 2 * p + hh
                            r.op("pe", lambda e, p=p, c=c, h=h, hh=hh, par=par: e.matmul(
                                psum[7][64 * hh:64 * (hh + 1), 64 * p:64 * (p + 1)],
                                ktm[:, 128 * par + 64 * hh:128 * par + 64 * (hh + 1)],
                                v_t[:, 1024 * c + 64 * h:1024 * c + 64 * (h + 1)],
                                start=False, stop=False,
                                tile_position=(0, 64) if hh else None))
                        r.tick_last("pe")
                    pe_last[5] = ev[len(kvseq) - 1]
                    pe_last[0] = ev[len(kvseq) - 1]
                    if j + 1 < NLT:
                        x_split(j + 1)
                else:
                    if j == 0:
                        r.wait("dve", kv_state["dma"])
                        r.op("dve", lambda e: e.memset(kv_car.all(), 0.0))
                        r.drain("dve")
                        for q in range(n_cc):
                            r.op("dve", lambda e, q=q: e.scalar_tensor_tensor(
                                kv_car.all(), r_tm4[:, 512 * q:512 * (q + 1)],
                                c_selcar[:, q:q + 1], kv_car.all(), OP.mult, OP.add))
                            r.drain("dve")
                        kv_state["tok"] = r.tick("dve", lambda e: e.tensor_copy(
                            kv_glob.all(), kv_car.all()))
                        r.drain("dve")
                    mixer_ltile(j)
                r.barrier()

            if not isB:
                grp(7, False)
                ptok = r.tick_last("pe")
                r.wait("dve", ptok)
                r.tick("dve", lambda e: e.tensor_copy(kv_tmp.all(), psum[7][:, 0:512]))
                pe_last[7] = r.dve_tok()
                for so_, car in ((0, car_k), (8, car_q), (16, car_kso), (24, car_qsi)):
                    r.drain("dve")
                    r.op("dve", lambda e, so_=so_: e.tensor_scalar_add(
                        blobtmp.all(), st_col[:, so_:so_ + 8], -EPS))
                    r.drain("dve")
                    r.op("dve", lambda e, car=car: e.tensor_copy(car.all(), blobtmp.all()))
                r.drain("dve")
                r.op("dve", lambda e: e.tensor_copy(car_cs.all(), st_cs[0:16, 0:1]))
                r.drain("dve")
                r.tick("dve", lambda e: e.nop())
                r.barrier()

        def mixer_ltile(j):
            st1 = {}
            pa_last = {}

            def stage1(c, p):
                par = p % 2
                sA, sB = (6, 1) if par == 0 else (2, 0)
                ob = 7 if par == 0 else 3
                r.wait("pe", pe_last.get(sA))
                r.wait("pe", pe_last.get(sB))
                r.wait("pe", pa_last.get(sB))
                r.op("pe", lambda e, p=p, c=c, sB=sB: e.transpose(
                    psum[sB][:, 128:256],
                    k_t[:, 512 * p + 128 * c:512 * p + 128 * (c + 1)], c_id128.all()))
                for hh in range(2):
                    r.op("pe", lambda e, p=p, c=c, hh=hh, sA=sA, sB=sB: e.matmul(
                        psum[sA if hh == 0 else sB][:, 0:128],
                        k_t[64 * hh:64 * (hh + 1), 512 * p + 128 * c:512 * p + 128 * (c + 1)],
                        x_t[64 * hh:64 * (hh + 1), 512 * p + 128 * c:512 * p + 128 * (c + 1)],
                        start=True, stop=True))
                sptok = r.tick_last("pe")
                obguard = pe_last.get(ob)
                r.wait("pe", obguard)
                r.wait("pe", pa_last.get(ob))
                grp(ob, True, cols=128)
                for hh in range(2):
                    r.op("pe", lambda e, p=p, c=c, hh=hh, ob=ob: e.matmul(
                        psum[ob][64 * hh:64 * (hh + 1), 0:128],
                        kv_glob[64 * hh:64 * (hh + 1), 64 * p:64 * (p + 1)],
                        x_t[64 * hh:64 * (hh + 1), 512 * p + 128 * c:512 * p + 128 * (c + 1)],
                        start=False, stop=False,
                        tile_position=(64, 64) if hh else None))
                r.wait("act", sptok)
                ktok = r.tick("act", lambda e, par=par, sB=sB: e.copy(
                    ktm[:, 128 * par:128 * (par + 1)], psum[sB][:, 128:256]))
                pa_last[sB] = ktok
                r.wait("dve", sptok)
                for hh in range(2):
                    r.op("dve", lambda e, hh=hh, par=par, sA=sA, sB=sB: e.tensor_tensor(
                        pr0[:, 256 * par + 128 * hh:256 * par + 128 * (hh + 1)],
                        psum[sA if hh == 0 else sB][:, 0:128], c_mask.all(), OP.mult))
                r.drain("dve")
                smtok = r.tick("dve", lambda e: e.nop())
                pe_last[sA] = smtok
                pe_last[sB] = smtok
                st1[p] = (smtok, ktok, ob, par, obguard)

            def stage2(c, p):
                smtok, ktok, ob, par, obguard = st1[p]
                r.wait("pe", smtok)
                for hh in range(2):
                    r.op("pe", lambda e, p=p, c=c, hh=hh, ob=ob, par=par: e.matmul(
                        psum[ob][64 * hh:64 * (hh + 1), 0:128],
                        v_t[:, 1024 * c + 128 * p + 64 * hh:1024 * c + 128 * p + 64 * (hh + 1)],
                        pr0[:, 256 * par + 128 * hh:256 * par + 128 * (hh + 1)],
                        start=False, stop=False,
                        tile_position=(0, 64) if hh else None))
                grp(ob, False, cols=128)
                otok = r.tick_last("pe")
                r.wait("pe", ktok)
                for hh in range(2):
                    r.op("pe", lambda e, p=p, c=c, hh=hh, par=par: e.matmul(
                        psum[4][64 * hh:64 * (hh + 1), 64 * p:64 * (p + 1)],
                        ktm[:, 128 * par + 64 * hh:128 * par + 64 * (hh + 1)],
                        v_t[:, 1024 * c + 128 * p + 64 * hh:1024 * c + 128 * p + 64 * (hh + 1)],
                        start=False, stop=False,
                        tile_position=(0, 64) if hh else None))
                bc16(ob, chain[0:16, 512 * 7 + 128 * c:512 * 7 + 128 * (c + 1)], p,
                     cols=128, o=128)
                btok = r.tick_last("pe")
                r.wait("act", btok)
                r.wait("act", obguard)
                sbtok = r.tick("act", lambda e, par=par, ob=ob: e.copy(
                    pr1[:, 128 * par:128 * (par + 1)], psum[ob][:, 128:256]))
                pa_last[ob] = sbtok
                r.wait("dve", sbtok)
                r.wait("dve", otok)
                hotok = r.tick("dve", lambda e, p=p, c=c, par=par, ob=ob: e.tensor_tensor(
                    k_t[:, 512 * p + 128 * c:512 * p + 128 * (c + 1)],
                    psum[ob][:, 0:128], pr1[:, 128 * par:128 * (par + 1)], OP.mult))
                pe_last[ob] = hotok

            if j == 0:
                r.wait("pe", kv_state["tok"])
                r.wait("dve", kv_state["tok"])
            for c in range(NCHT):
                r.wait("pe", vs_state[c])
                r.wait("pe", pe_last.get(4))
                grp(4, True)
                for idx in range(NPAIR + 1):
                    if idx < NPAIR:
                        stage1(c, idx)
                    if idx > 0:
                        stage2(c, idx - 1)
                grp(4, False)
                cltok = r.tick_last("pe")
                r.wait("dve", cltok)
                r.tick("dve", lambda e: e.tensor_tensor(kv_tmp.all(), psum[4][:, 0:512],
                                                        kv_glob.all(), OP.add))
                r.drain("dve")
                r.op("dve", lambda e: e.tensor_copy(kv_glob.all(), kv_tmp.all()))
                r.drain("dve")
                gtok = r.tick("dve", lambda e: e.nop())
                pe_last[4] = gtok
            pe_last[5] = r.dve_tok()

            mg_pe = r.pe_tok()
            mg_dve = r.dve_tok()
            r.wait("sync", mg_pe)
            r.wait("sync", mg_dve)
            for tb in range(4):
                r.dma_op("sync", lambda e, tb=tb, j=j: e.dma_start(
                    out=r_tm4[:, 1024 * tb:1024 * (tb + 1)],
                    in_=dt["x_tm"].ap()[128 * (4 * j + tb):128 * (4 * j + tb + 1), :]))
            xttok = r.bump_after_dma("sync")
            r.wait("act", mg_pe)
            r.wait("act", mg_dve)
            for b in range(2):
                atok = r.tick("act", lambda e, b=b: e.copy(
                    q16_hi[:, 2048 * b:2048 * (b + 1)], k_t[:, 2048 * b:2048 * (b + 1)]))
                r.wait("dve", atok)
                r.op("dve", lambda e, b=b: e.tensor_tensor(
                    q16_lo[:, 2048 * b:2048 * (b + 1)], k_t[:, 2048 * b:2048 * (b + 1)],
                    q16_hi[:, 2048 * b:2048 * (b + 1)], OP.subtract))
            r.drain("dve")
            k16tok = r.tick("dve", lambda e: e.nop())
            r.wait("pe", k16tok)
            r.wait("pe", wo_state["ready"])
            for bk in range(8):
                r.wait("pe", pe_last.get(bk))
            for pas in range(3):
                lv, wv = ((q16_hi, wo16_hi), (q16_lo, wo16_hi), (q16_hi, wo16_lo))[pas]
                for d in range(8):
                    for tb in range(4):
                        for nh in range(2):
                            r.op("pe", lambda e, lv=lv, wv=wv, d=d, tb=tb, nh=nh,
                                 pas=pas: e.matmul(
                                psum[2 * tb + nh][:, 0:512],
                                lv[:, 512 * d + 128 * tb:512 * d + 128 * (tb + 1)],
                                wv[:, 1024 * d + 512 * nh:1024 * d + 512 * (nh + 1)],
                                start=(pas == 0 and d == 0), stop=False))
            for tb in range(4):
                for nh in range(2):
                    r.op("pe", lambda e, nh=nh, tb=tb: e.matmul(
                        psum[2 * tb + nh][:, 0:512], c_ones32.all(),
                        c_bo[0:1, 512 * nh:512 * (nh + 1)], start=False, stop=True))
            ptok = r.tick_last("pe")
            r.wait("dve", xttok)
            r.wait("dve", ptok)
            for tb in range(4):
                for nh in range(2):
                    etok = r.tick("dve", lambda e, tb=tb, nh=nh: e.tensor_tensor(
                        r_tm4[:, 1024 * tb + 512 * nh:1024 * tb + 512 * (nh + 1)],
                        psum[2 * tb + nh][:, 0:512],
                        r_tm4[:, 1024 * tb + 512 * nh:1024 * tb + 512 * (nh + 1)], OP.add))
                    pe_last[2 * tb + nh] = etok
            r.drain("dve")
            for tb in range(4):
                ln_norm(4 * j + tb, tb)

        def ln_norm(tbg, tb):
            rt = View(arena, WR + 1024 * tb, 1024)
            r.op("dve", lambda e, rt=rt: e.bn_stats(ln_sml[:, 0:6], rt[:, 0:512]))
            r.op("dve", lambda e, rt=rt: e.bn_stats(ln_sml[:, 6:12], rt[:, 512:1024]))
            r.drain("dve")
            r.op("dve", lambda e: e.bn_aggr(ln_sml[:, 12:14], ln_sml[:, 0:12]))
            r.drain("dve")
            r.op("dve", lambda e: e.tensor_scalar_add(ln_sml[:, 14:15], ln_sml[:, 13:14], 1e-5))
            r.drain("dve")
            stok = r.tick("dve", lambda e: e.nop())
            r.wait("act", stok)
            atok = r.tick("act", lambda e: e.activation(ln_sml[:, 15:16], ln_sml[:, 14:15], AF.Sqrt))
            r.wait("dve", atok)
            r.op("dve", lambda e: e.reciprocal(ln_sml[:, 16:17], ln_sml[:, 15:16]))
            r.drain("dve")
            lntok = r.tick("dve", lambda e, rt=rt: e.tensor_scalar(
                rt.all(), rt.all(), ln_sml[:, 12:13], ln_sml[:, 16:17], OP.subtract, OP.mult))
            r.drain("dve")
            r.wait("sync", lntok)
            r.dma_op("sync", lambda e, tbg=tbg, rt=rt: e.dma_start(
                out=scr_tm.ap()[128 * tbg:128 * (tbg + 1), :], in_=rt.all()))
            r.wait("pe", lntok)
            r.wait("pe", pe_last.get(6))
            lnev = {}

            def ln_t(f):
                bk = 4 + f % 2
                r.wait("pe", pe_last.get(bk))
                r.op("pe", lambda e, f=f, bk=bk, rt=rt: e.transpose(
                    psum[bk][:, 0:128], rt[:, 128 * f:128 * (f + 1)], c_id128.all()))
                return r.tick_last("pe")

            def ln_gate(f):
                r.wait("pe", lnev[f])
                r.op("pe", lambda e, f=f: e.matmul(
                    psum[6][:, 0:8], lnb_hi[:, 128 * f:128 * (f + 1)],
                    gwt_hi[:, 8 * f:8 * (f + 1)], start=(f == 0), stop=False))
                r.op("pe", lambda e, f=f: e.matmul(
                    psum[6][:, 0:8], lnb_lo[:, 128 * f:128 * (f + 1)],
                    gwt_hi[:, 8 * f:8 * (f + 1)], start=False, stop=False))
                r.op("pe", lambda e, f=f: e.matmul(
                    psum[6][:, 0:8], lnb_hi[:, 128 * f:128 * (f + 1)],
                    gwt_lo[:, 8 * f:8 * (f + 1)], start=False, stop=False))

            ptks = {0: ln_t(0)}
            for f in range(8):
                bk = 4 + f % 2
                r.wait("act", ptks[f])
                htok = r.tick("act", lambda e, f=f, bk=bk: e.copy(
                    lnb_hi[:, 128 * f:128 * (f + 1)], psum[bk][:, 0:128]))
                r.wait("dve", htok)
                etok = r.tick("dve", lambda e, f=f, bk=bk: e.tensor_tensor(
                    lnb_lo[:, 128 * f:128 * (f + 1)], psum[bk][:, 0:128],
                    lnb_hi[:, 128 * f:128 * (f + 1)], OP.subtract))
                pe_last[bk] = etok
                lnev[f] = etok
                if f + 1 < 8:
                    ptks[f + 1] = ln_t(f + 1)
                if f > 0:
                    ln_gate(f - 1)
                r.wait("sync", etok)
                r.dma_op("sync", lambda e, f=f, tbg=tbg: e.dma_start(
                    out=scr16_hi.ap()[128 * f:128 * (f + 1), 128 * tbg:128 * (tbg + 1)],
                    in_=lnb_hi[:, 128 * f:128 * (f + 1)]))
                r.dma_op("sync", lambda e, f=f, tbg=tbg: e.dma_start(
                    out=scr16_lo.ap()[128 * f:128 * (f + 1), 128 * tbg:128 * (tbg + 1)],
                    in_=lnb_lo[:, 128 * f:128 * (f + 1)]))
            ln_gate(7)
            r.op("pe", lambda e: e.matmul(psum[6][:, 0:8], c_ones1[0:1, 0:128], c_lb.all(),
                                          start=False, stop=True))
            gptok = r.tick_last("pe")
            r.wait("act", gptok)
            sctok = r.tick("act", lambda e: e.activation(g_sml[:, 0:8], psum[6][:, 0:8], AF.Sigmoid))
            pe_last[6] = sctok
            gating_tb(tbg, sctok)
            r.wait("sync", r.dve_tok())
            sb = r.bump_after_dma("sync")
            r.wait("dve", sb)

        def gating_tb(tbg, sctok):
            sc_ = g_sml[:, 0:8]
            s1, s2 = g_sml[:, 8:9], g_sml[:, 9:10]
            w1, w2, ws = g_sml[:, 10:11], g_sml[:, 11:12], g_sml[:, 12:13]
            m1, m2 = g_sml[:, 16:24], g_sml[:, 24:32]
            pf, t8, sel = g_sml[:, 32:40], g_sml[:, 40:48], g_sml[:, 48:56]
            sumc = g_sml[:, 56:64]

            def d(fn):
                r.op("dve", fn)
                r.drain("dve")

            r.wait("dve", sctok)
            d(lambda e: e.reduce_max(s1, sc_, axis=AX))
            d(lambda e: e.tensor_scalar(m1, sc_, s1, None, OP.is_ge))
            d(lambda e: e.tensor_tensor_scan(pf, m1, m1, 0.0, OP.add, OP.bypass))
            d(lambda e: e.tensor_single_scalar(t8, pf, 1.5, OP.is_le))
            d(lambda e: e.tensor_tensor(m1, m1, t8, OP.mult))
            d(lambda e: e.scalar_tensor_tensor(t8, m1, -2.0, sc_, OP.mult, OP.add))
            d(lambda e: e.reduce_max(s2, t8, axis=AX))
            d(lambda e: e.tensor_scalar(m2, t8, s2, None, OP.is_ge))
            d(lambda e: e.tensor_tensor_scan(pf, m2, m2, 0.0, OP.add, OP.bypass))
            d(lambda e: e.tensor_single_scalar(t8, pf, 1.5, OP.is_le))
            d(lambda e: e.tensor_tensor(m2, m2, t8, OP.mult))
            d(lambda e: e.tensor_tensor(sel, m1, m2, OP.add))
            d(lambda e: e.tensor_tensor(ws, s1, s2, OP.add))
            d(lambda e: e.tensor_scalar_add(ws, ws, 1e-8))
            d(lambda e: e.reciprocal(ws, ws))
            d(lambda e: e.tensor_tensor(w1, s1, ws, OP.mult))
            d(lambda e: e.tensor_tensor(w2, s2, ws, OP.mult))
            d(lambda e: e.tensor_scalar(sumc, m1, w1, None, OP.mult))
            d(lambda e: e.tensor_scalar(t8, m2, w2, None, OP.mult))
            d(lambda e: e.tensor_tensor(sumc, sumc, t8, OP.add))
            d(lambda e: e.tensor_tensor(snacc[:, 0:8], snacc[:, 0:8], sumc, OP.add))
            d(lambda e: e.tensor_tensor(snacc[:, 8:16], snacc[:, 8:16], sel, OP.add))
            r.tick("dve", lambda e, tbg=tbg: e.tensor_copy(sel_all[:, 8 * tbg:8 * (tbg + 1)], sel))

        marks = []

        def mark():
            marks.append({e: len(r.ops[e]) for e in ENGS})

        mark()
        attention_pass("A")
        mark()

        toks = [r.dma_op("pool", lambda e: e.dma_start(
            out=ag_in.ap()[0:65536].rearrange("(p c) -> p c", p=128), in_=kv_tmp.all()))]
        toks.append(r.dma_op("pool", lambda e: e.dma_start(
            out=ag_in.ap()[65536:69632].rearrange("(p c) -> p c", p=128),
            in_=arena[0:128, car_k.off:car_k.off + 32])))
        toks.append(r.dma_op("pool", lambda e: e.dma_start(
            out=ag_in.ap()[69632:69648].rearrange("(p c) -> p c", p=16), in_=car_cs.all())))
        r.bump_after_dma("pool")
        cc1tok = r.cc_op(lambda e: e.collective_compute(
            "AllGather", OP.bypass, replica_groups=[list(range(n_cc))],
            ins=[ag_in.ap()[65536:69648].opt()],
            outs=[ag_out.ap()[65536 * n_cc:65536 * n_cc + 4112 * n_cc].opt()]))
        cc2tok = r.cc_op(lambda e: e.collective_compute(
            "AllGather", OP.bypass, replica_groups=[list(range(n_cc))],
            ins=[ag_in.ap()[0:65536].opt()], outs=[ag_out.ap()[0:65536 * n_cc].opt()]))
        kv_state["cc2"] = cc2tok
        for src, dst in ((scr_q, q_t), (scr_k, k_t), (scr_v, v_t)):
            r.dma_op("sync", lambda e, src=src, dst=dst: e.dma_start(
                out=dst.all(), in_=src.ap()[:, 0:4096]))
        hoist["rl0"] = r.bump_after_dma("sync")
        r.wait("pool", cc1tok)
        r.wait("dve", cc1tok)
        for q in range(n_cc):
            r.dma_op("pool", lambda e, q=q: e.dma_start(
                out=slotc[:, 32 * q:32 * (q + 1)],
                in_=ag_out.ap()[65536 * n_cc + q * 4112:65536 * n_cc + q * 4112 + 4096]
                .rearrange("(p c) -> p c", p=128)))
            r.dma_op("pool", lambda e, q=q: e.dma_start(
                out=slotcs[0:16, q:q + 1],
                in_=ag_out.ap()[65536 * n_cc + q * 4112 + 4096:65536 * n_cc + q * 4112 + 4112]
                .rearrange("(p c) -> p c", p=16)))
        ftok = r.bump_after_dma("pool")
        r.wait("dve", ftok)
        r.op("dve", lambda e: e.memset(kv_car.all(), 0.0))
        r.op("dve", lambda e: e.memset(car_k.all(), 0.0))
        r.op("dve", lambda e: e.memset(car_q.all(), 0.0))
        r.op("dve", lambda e: e.memset(car_kso.all(), 0.0))
        r.op("dve", lambda e: e.memset(car_qsi.all(), 0.0))
        r.op("dve", lambda e: e.memset(car_cs.all(), 0.0))
        r.drain("dve")
        for q in range(n_cc):
            r.op("dve", lambda e, q=q: e.scalar_tensor_tensor(
                arena[0:128, car_k.off:car_k.off + 32],
                slotc[:, 32 * q:32 * (q + 1)], c_selcar[:, q:q + 1],
                arena[0:128, car_k.off:car_k.off + 32], OP.mult, OP.add))
            r.drain("dve")
            r.op("dve", lambda e, q=q: e.scalar_tensor_tensor(
                car_cs.all(), slotcs[0:16, q:q + 1], c_selcar[0:16, q:q + 1],
                car_cs.all(), OP.mult, OP.add))
            r.drain("dve")
        r.tick("dve", lambda e: e.nop())
        r.barrier()
        mark()

        attention_pass("B")
        mark()

        r.wait("pe", pe_last.get(0))
        r.op("pe", lambda e: e.matmul(psum[0][0:1, 0:16], c_onescol.all(), snacc.all(),
                                      start=True, stop=True))
        ptok = r.tick_last("pe")
        r.wait("dve", ptok)
        r.tick("dve", lambda e: e.tensor_copy(g_sml[0:1, 64:80], psum[0][0:1, 0:16]))
        pe_last[0] = r.dve_tok()
        r.drain("dve")
        dtok = r.dve_tok()
        r.wait("pool", dtok)
        r.dma_op("pool", lambda e: e.dma_start(out=ar_in.ap()[0:1, :], in_=g_sml[0:1, 64:80]))
        r.bump_after_dma("pool")
        cctok = r.cc_op(lambda e: e.collective_compute(
            "AllReduce", OP.add, replica_groups=[list(range(n_cc))],
            ins=[ar_in.ap().opt()], outs=[ar_out.ap().opt()]))
        mark()
        r.wait("sync", r.pe_tok())
        r.wait("sync", r.dve_tok())
        r.wait("sync", split_state["spill"])
        r.wait("sync", split_state["sub"])
        ffn_dve_fence = ("act", r.cnt["act"])
        ar_state = {"catok": None}

        def ar_tail():
            r.wait("pool", cctok)
            r.dma_op("pool", lambda e: e.dma_start(out=g_sml[0:1, 64:80], in_=ar_out.ap()[0:1, :]))
            r.await_dma("dve", "pool")
            r.op("dve", lambda e: e.tensor_single_scalar(g_sml[0:1, 80:88], g_sml[0:1, 72:80], 1.0, OP.max))
            r.drain("dve")
            r.op("dve", lambda e: e.reciprocal(g_sml[0:1, 88:96], g_sml[0:1, 80:88]))
            r.drain("dve")
            r.op("dve", lambda e: e.tensor_tensor(g_sml[0:1, 80:88], g_sml[0:1, 64:72],
                                                  g_sml[0:1, 88:96], OP.mult))
            r.drain("dve")
            wtok = r.tick("dve", lambda e: e.nop())
            r.wait("pe", wtok)
            r.wait("pe", pe_last.get(1))
            r.op("pe", lambda e: e.matmul(psum[1][:, 0:8], c_ones1[0:1, 0:128], g_sml[0:1, 80:88],
                                          start=True, stop=True))
            ptok = r.tick_last("pe")
            r.wait("dve", ptok)
            r.tick("dve", lambda e: e.tensor_copy(wm_bc.all(), psum[1][:, 0:8]))
            pe_last[1] = r.dve_tok()
            r.drain("dve")
            for tbg in range(NTB):
                r.op("dve", lambda e, tbg=tbg: e.tensor_tensor(
                    c_all[:, 8 * tbg:8 * (tbg + 1)], sel_all[:, 8 * tbg:8 * (tbg + 1)],
                    wm_bc.all(), OP.mult))
            r.drain("dve")
            ar_state["catok"] = r.tick("dve", lambda e: e.nop())

        ln_pref = {}

        def prefetch_ln(tt, war_tok):
            b = tt % 2
            if war_tok is not None:
                r.wait("sync", war_tok)
            for m in range(8):
                r.dma_op("sync", lambda e, tt=tt, b=b, m=m: e.dma_start(
                    out=f_lnh[b][:, 512 * m:512 * (m + 1)],
                    in_=scr16_hi.ap()[128 * m:128 * (m + 1), 512 * tt:512 * (tt + 1)]))
                r.dma_op("sync", lambda e, tt=tt, b=b, m=m: e.dma_start(
                    out=f_lnl[b][:, 512 * m:512 * (m + 1)],
                    in_=scr16_lo.ap()[128 * m:128 * (m + 1), 512 * tt:512 * (tt + 1)]))
            ln_pref[tt] = (b, r.bump_after_dma("sync"))

        prefetch_ln(0, None)
        WSETS = (((f_wgh, f_wgl), (f_wuh, f_wul)), ((f2_wgh, f2_wgl), (f2_wuh, f2_wul)))
        wfence = {}
        dw_last_use = [None]

        def prefetch_gu(e_i):
            st_ = WSETS[e_i % 2]
            for mi, (dsth, dstl) in enumerate(st_):
                idx = 6 * e_i + 2 * mi
                r.dma_op("sync", lambda e, idx=idx, d=dsth: e.dma_start(
                    out=d.all(), in_=ffn16.ap()[idx, :, :]))
                r.dma_op("sync", lambda e, idx=idx, d=dstl: e.dma_start(
                    out=d.all(), in_=ffn16.ap()[idx + 1, :, :]))
            wfence[e_i] = r.bump_after_dma("sync")

        prefetch_gu(0)
        for e_i in range(NEXP):
            (wgh, wgl), (wuh, wul) = WSETS[e_i % 2]
            r.wait("sync", dw_last_use[0])
            idx = 6 * e_i + 4
            r.dma_op("sync", lambda e, idx=idx: e.dma_start(
                out=f_wdh.all(), in_=ffn16.ap()[idx, :, :]))
            r.dma_op("sync", lambda e, idx=idx: e.dma_start(
                out=f_wdl.all(), in_=ffn16.ap()[idx + 1, :, :]))
            dwf = r.bump_after_dma("sync")
            r.wait("pe", wfence[e_i])
            if e_i + 1 < NEXP:
                prefetch_gu(e_i + 1)
            for tt in range(NLT):
                b, ftok = ln_pref[tt]
                r.wait("pe", ftok)
                lnh, lnl = f_lnh[b], f_lnl[b]
                h_tok = {}

                def h_block(ib):
                    bg, bu = ib % 2, 2 + ib % 2
                    r.wait("pe", pe_last.get(bg))
                    r.wait("pe", pe_last.get(bu))
                    for wgt_h, wgt_l, bk in ((wgh, wgl, bg), (wuh, wul, bu)):
                        for m in range(8):
                            r.op("pe", lambda e, m=m, ib=ib, bk=bk, wh=wgt_h, lh=lnh: e.matmul(
                                psum[bk][:, 0:512],
                                wh[:, 512 * m + 128 * ib:512 * m + 128 * (ib + 1)],
                                lh[:, 512 * m:512 * (m + 1)], start=(m == 0), stop=False))
                            r.op("pe", lambda e, m=m, ib=ib, bk=bk, wh=wgt_h, ll=lnl: e.matmul(
                                psum[bk][:, 0:512],
                                wh[:, 512 * m + 128 * ib:512 * m + 128 * (ib + 1)],
                                ll[:, 512 * m:512 * (m + 1)], start=False, stop=False))
                            r.op("pe", lambda e, m=m, ib=ib, bk=bk, wl=wgt_l, lh=lnh: e.matmul(
                                psum[bk][:, 0:512],
                                wl[:, 512 * m + 128 * ib:512 * m + 128 * (ib + 1)],
                                lh[:, 512 * m:512 * (m + 1)], start=False,
                                stop=(m == 7)))
                    ptok = r.tick_last("pe")
                    r.wait("act", ptok)
                    stok = r.tick("act", lambda e, bg=bg: e.activation(
                        f_si.all(), psum[bg][:, 0:512], AF.Silu))
                    r.wait("dve", stok)
                    r.drain("dve")
                    r.op("dve", lambda e, bu=bu: e.tensor_tensor(
                        f_hf.all(), f_si.all(), psum[bu][:, 0:512], OP.mult))
                    r.drain("dve")
                    hftok = r.tick("dve", lambda e: e.nop())
                    r.wait("act", hftok)
                    hitok = r.tick("act", lambda e, ib=ib: e.copy(
                        f_hh[:, 512 * ib:512 * (ib + 1)], f_hf.all()))
                    r.wait("dve", hitok)
                    htok = r.tick("dve", lambda e, ib=ib: e.tensor_tensor(
                        f_hl[:, 512 * ib:512 * (ib + 1)], f_hf.all(),
                        f_hh[:, 512 * ib:512 * (ib + 1)], OP.subtract))
                    pe_last[bg] = stok
                    pe_last[bu] = htok
                    h_tok[ib] = htok

                def down_part(tbs, ihs):
                    for tb in tbs:
                        for nh in range(2):
                            bk = 4 + 2 * (tb % 2) + nh
                            if ihs[0] == 0:
                                r.wait("pe", pe_last.get(bk))
                            for ih in ihs:
                                r.op("pe", lambda e, ih=ih, tb=tb, nh=nh, bk=bk: e.matmul(
                                    psum[bk][:, 0:512],
                                    f_hh[:, 512 * ih + 128 * tb:512 * ih + 128 * (tb + 1)],
                                    f_wdh[:, 1024 * ih + 512 * nh:1024 * ih + 512 * (nh + 1)],
                                    start=(ih == 0), stop=False))
                                r.op("pe", lambda e, ih=ih, tb=tb, nh=nh, bk=bk: e.matmul(
                                    psum[bk][:, 0:512],
                                    f_hl[:, 512 * ih + 128 * tb:512 * ih + 128 * (tb + 1)],
                                    f_wdh[:, 1024 * ih + 512 * nh:1024 * ih + 512 * (nh + 1)],
                                    start=False, stop=False))
                                r.op("pe", lambda e, ih=ih, tb=tb, nh=nh, bk=bk: e.matmul(
                                    psum[bk][:, 0:512],
                                    f_hh[:, 512 * ih + 128 * tb:512 * ih + 128 * (tb + 1)],
                                    f_wdl[:, 1024 * ih + 512 * nh:1024 * ih + 512 * (nh + 1)],
                                    start=False, stop=(ih == 3)))

                def evac(tbs):
                    ptok = r.tick_last("pe")
                    if e_i == 0 and tt == 0:
                        r.wait("dve", ar_state["catok"])
                    for tb in tbs:
                        tbg = 4 * tt + tb
                        r.wait("dve", ptok)
                        for nh in range(2):
                            bk = 4 + 2 * (tb % 2) + nh
                            ccol = (c_all[:, 8 * tbg + e_i:8 * tbg + e_i + 1] if e_i < 8
                                    else c_one[:, 0:1])
                            if e_i == 0:
                                etok = r.tick("dve", lambda e, bk=bk, tbg=tbg, nh=nh, ccol=ccol:
                                              e.tensor_scalar(
                                                  acc[:, 1024 * tbg + 512 * nh:1024 * tbg + 512 * (nh + 1)],
                                                  psum[bk][:, 0:512], ccol, None, OP.mult))
                            else:
                                etok = r.tick("dve", lambda e, bk=bk, tbg=tbg, nh=nh, ccol=ccol:
                                              e.scalar_tensor_tensor(
                                                  acc[:, 1024 * tbg + 512 * nh:1024 * tbg + 512 * (nh + 1)],
                                                  psum[bk][:, 0:512], ccol,
                                                  acc[:, 1024 * tbg + 512 * nh:1024 * tbg + 512 * (nh + 1)],
                                                  OP.mult, OP.add))
                            pe_last[bk] = etok
                        r.drain("dve")
                    return ptok

                for ib in range(4):
                    h_block(ib)
                if e_i == 0 and tt == 0:
                    ar_tail()
                    r.wait("dve", ffn_dve_fence)
                if tt == 0:
                    r.wait("pe", dwf)
                for ib in range(4):
                    r.wait("pe", h_tok[ib])
                nxt = tt + 1 if tt + 1 < NLT else (0 if e_i + 1 < NEXP else None)
                if nxt is not None:
                    prefetch_ln(nxt, r.pe_tok())
                for tb in range(4):
                    down_part((tb,), (0, 1))
                    down_part((tb,), (2, 3))
                    tile_done = evac((tb,))
                r.wait("sync", tile_done)
                if tt == NLT - 1:
                    dw_last_use[0] = tile_done

        for tbg in range(NTB):
            r.dma_op("sync", lambda e, tbg=tbg: e.dma_start(
                out=x_tm_sb.all(), in_=scr_tm.ap()[128 * tbg:128 * (tbg + 1), :]))
            r.await_dma("dve", "sync")
            r.op("dve", lambda e, tbg=tbg: e.tensor_tensor(
                r_tm.all(), acc[:, 1024 * tbg:1024 * (tbg + 1)], x_tm_sb.all(), OP.add))
            r.drain("dve")
            r.op("dve", lambda e: e.bn_stats(ln_sml[:, 0:6], r_tm[:, 0:512]))
            r.op("dve", lambda e: e.bn_stats(ln_sml[:, 6:12], r_tm[:, 512:1024]))
            r.drain("dve")
            r.op("dve", lambda e: e.bn_aggr(ln_sml[:, 12:14], ln_sml[:, 0:12]))
            r.drain("dve")
            r.op("dve", lambda e: e.tensor_scalar_add(ln_sml[:, 14:15], ln_sml[:, 13:14], 1e-5))
            r.drain("dve")
            stok = r.tick("dve", lambda e: e.nop())
            r.wait("act", stok)
            atok = r.tick("act", lambda e: e.activation(ln_sml[:, 15:16], ln_sml[:, 14:15], AF.Sqrt))
            r.wait("dve", atok)
            r.op("dve", lambda e: e.reciprocal(ln_sml[:, 16:17], ln_sml[:, 15:16]))
            r.drain("dve")
            otok = r.tick("dve", lambda e: e.tensor_scalar(
                r_tm.all(), r_tm.all(), ln_sml[:, 12:13], ln_sml[:, 16:17], OP.subtract, OP.mult))
            r.drain("dve")
            r.wait("sync", otok)
            r.dma_op("sync", lambda e, tbg=tbg: e.dma_start(
                out=dt["out"].ap()[128 * tbg:128 * (tbg + 1), :], in_=r_tm.all()))
        r.wait("sync", ("sync_dma", r.dma["sync"]))
        r.wait("pool", ("pool_dma", r.dma["pool"]))

        if STOP > 0:
            m = marks[STOP - 1]
            for e in ENGS:
                r.ops[e] = r.ops[e][:m[e]]
            for e in ENGS:
                cnt = 0
                for kind, fn, semn, val in r.ops[e]:
                    if kind == "dma":
                        cnt = val
                r.dma[e] = cnt
            for tbg in range(NTB):
                r.dma_op("sync", lambda e, tbg=tbg: e.dma_start(
                    out=dt["out"].ap()[128 * tbg:128 * (tbg + 1), :], in_=x_tm_sb.all()))
            r.wait("sync", ("sync_dma", r.dma["sync"]))
            r.wait("pool", ("pool_dma", r.dma["pool"]))

        with nc.Block() as blk:
            def emit(eng_name):
                def body(e):
                    for kind, fn, semn, val in r.ops[eng_name]:
                        if kind == "op":
                            fn(e)
                        elif kind == "inc":
                            fn(e).then_inc(sems[semn], 1)
                        elif kind == "dma":
                            fn(e).then_inc(sems[semn], 16)
                        elif kind == "bump":
                            e.sem_inc(sems[semn], 1)
                        elif kind == "wait":
                            e.wait_ge(sems[semn], val)
                        elif kind == "drain":
                            e.drain()
                return body

            blk.tensor(emit("pe"))
            blk.scalar(emit("act"))
            blk.vector(emit("dve"))
            blk.gpsimd(emit("pool"))
            blk.sync(emit("sync"))

    return nc



def _constants(core, n_cc=NCORES):
    b, g = divmod(core, 2)
    offp = g * LH
    pos = np.arange(1, LH + 1, dtype=np.float64)
    norm = np.zeros((48, LH), dtype=np.float32)
    norm[0:16] = (pos + offp).astype(np.float32)
    norm[32:48] = (1.0 / (pos + offp)).astype(np.float32)
    jl = (np.arange(128)[:, None] <= np.arange(128)[None, :]).astype(np.float32)
    selcar = np.zeros((128, n_cc), dtype=np.float32)
    if g == 1 and n_cc == NCORES:
        selcar[:, core - 1] = 1.0
    obd16 = np.zeros((128, 128), dtype=np.float32)
    selp16 = np.zeros((16, 1024), dtype=np.float32)
    for p in range(8):
        obd16[0:64, 16 * p + 2 * p] = 1.0
        obd16[64:128, 16 * p + 2 * p + 1] = 1.0
        selp16[2 * p, 128 * p:128 * p + 64] = 1.0
        selp16[2 * p + 1, 128 * p + 64:128 * (p + 1)] = 1.0
    return {
        "c_id128": np.eye(128, dtype=np.float32),
        "c_mask": jl.copy(),
        "c_obd16": obd16,
        "c_selp16": selp16,
        "c_ones1": np.ones((1, 128), dtype=np.float32),
        "c_onescol": np.ones((128, 1), dtype=np.float32),
        "c_norm": norm,
        "c_selcar": selcar,
        "c_one": np.ones((128, 1), dtype=np.float32),
    }


def _stage(core, inp, n_cc=NCORES):
    b, g = divmod(core, 2)
    sl = slice(g * LH, (g + 1) * LH)
    x = np.asarray(inp["L0_attn_x"], np.float32)[b, sl]
    d = {
        "x_fm": np.ascontiguousarray(x.T),
        "x_tm": np.ascontiguousarray(x),
        "w_q": np.ascontiguousarray(np.asarray(inp["L0_attn_Wq"], np.float32)),
        "w_k": np.ascontiguousarray(np.asarray(inp["L0_attn_Wk"], np.float32)),
        "w_v": np.ascontiguousarray(np.asarray(inp["L0_attn_Wv"], np.float32)),
        "w_o": np.ascontiguousarray(np.asarray(inp["L0_attn_Wo"], np.float32)),
        "b_q": np.ascontiguousarray(np.asarray(inp["L0_attn_bq"], np.float32).reshape(8, 128).T),
        "b_k": np.ascontiguousarray(np.asarray(inp["L0_attn_bk"], np.float32).reshape(8, 128).T),
        "b_v": np.asarray(inp["L0_attn_bv"], np.float32).reshape(1, D),
        "b_o": np.asarray(inp["L0_attn_bo"], np.float32).reshape(1, D),
        "gw_t": np.ascontiguousarray(np.asarray(inp["L0_ffn_gate_w"], np.float32).T),
        "lb": np.asarray(inp["L0_ffn_lb_bias"], np.float32).reshape(1, 8),
    }
    eg = np.asarray(inp["L0_ffn_expert_gate_w"], np.float32)
    eu = np.asarray(inp["L0_ffn_expert_up_w"], np.float32)
    ed = np.asarray(inp["L0_ffn_expert_down_w"], np.float32)
    sg = np.asarray(inp["L0_ffn_shared_gate_w"], np.float32)
    su = np.asarray(inp["L0_ffn_shared_up_w"], np.float32)
    sd = np.asarray(inp["L0_ffn_shared_down_w"], np.float32)
    gw = np.empty((NEXP, D, IEXP), dtype=np.float32)
    uw = np.empty((NEXP, D, IEXP), dtype=np.float32)
    dw = np.empty((NEXP, IEXP, D), dtype=np.float32)
    for e in range(8):
        gw[e] = eg[e].T; uw[e] = eu[e].T; dw[e] = ed[e].T
    gw[8] = sg[0:512].T; gw[9] = sg[512:1024].T
    uw[8] = su[0:512].T; uw[9] = su[512:1024].T
    dw[8] = sd[:, 0:512].T; dw[9] = sd[:, 512:1024].T
    d["ffn_gw"] = gw; d["ffn_uw"] = uw; d["ffn_dw"] = dw
    d.update(_constants(core, n_cc))
    return d


def kernel(**inputs):
    nc = build_nc()
    in_maps = [_stage(c, inputs) for c in range(NCORES)]
    res = run_bass_kernel_spmd(nc, in_maps, core_ids=list(range(NCORES)))
    out = np.empty((B, L, D), dtype=np.float32)
    for c in range(NCORES):
        b, g = divmod(c, 2)
        out[b, g * LH:(g + 1) * LH] = res.results[c]["out"]
    return out
